# version 04 (final)
# speedup vs baseline: 1.5943x; 1.0009x over previous
import math
from contextlib import ExitStack

import numpy as np
import ml_dtypes
import concourse.bass as bass
import concourse.mybir as mybir
from concourse.bass_utils import run_bass_kernel_spmd
from concourse.bass_core_rust import IndirectOffsetOnAxis

F32 = mybir.dt.float32
BF16 = mybir.dt.bfloat16
I32 = mybir.dt.int32
AF = mybir.ActivationFunctionType
ALU = mybir.AluOpType
AX = mybir.AxisListType.X
BF = ml_dtypes.bfloat16

B, S, D = 4, 2048, 2048
TPC = 1024
NT = TPC // 128
DC = D // 128
LAMBDA_INIT = 0.8 - 0.6 * float(np.exp(np.float32(np.float32(-0.3) * np.float32(12))))
LN2 = float(np.log(2.0))
INV_LN2 = float(1.0 / np.log(2.0))

ENGINES = ("sync", "gpsimd", "vector", "scalar", "tensor")

C_LNEPS, C_RMSEPS, C_L2EPS, C_HALF, C_B4LN2, C_M3LN2, C_GNEPS, C_ONE = range(8)
CONST_VALS = [1e-5, 1e-8, 1e-8, 0.5, 3.0 * LN2, -3.0 * LN2, 1e-5, 1.0, 1e-30]


def chunk_rows(core):
    if core % 2 == 0:
        return [(0, 512), (1536, 2048)]
    return [(512, 1536)]


def qtile_bases(core):
    bases = []
    for lo, hi in chunk_rows(core):
        for b0 in range(lo, hi, 128):
            bases.append(b0)
    return bases


class Ctx:

    def __init__(self, nc):
        self.nc = nc
        self.q = {e: [] for e in ENGINES}
        self.sems = {}
        self.cnt = {}
        self.es = ExitStack()
        self._scopes = []

    def push_scope(self):
        self._scopes.append(ExitStack())

    def pop_scope(self):
        self._scopes.pop().close()

    def barrier_wait(self):
        return [(s, self.cnt.get(s, 0)) for s in ("pe", "dv", "ac", "po")
                if self.cnt.get(s, 0) > 0]

    def sem(self, name):
        if name not in self.sems:
            self.sems[name] = self.es.enter_context(self.nc.semaphore(name))
            self.cnt[name] = 0
        return self.sems[name]

    def sb(self, name, shape, dtype=F32):
        tgt = self._scopes[-1] if self._scopes else self.es
        return tgt.enter_context(self.nc.sbuf_tensor(name, shape, dtype))

    def ps(self, name, shape, dtype=F32):
        tgt = self._scopes[-1] if self._scopes else self.es
        return tgt.enter_context(self.nc.psum_tensor(name, shape, dtype))

    AUTO = {"vector": "dv", "scalar": "ac", "gpsimd": "po", "tensor": "pe"}

    def op(self, eng, fn, wait=(), inc=None, inc_n=1, dma=False, chain=True):
        wait = list(wait)
        for s, _ in wait:
            self.sem(s)
        auto = None
        if not dma and eng in self.AUTO and chain:
            auto = self.AUTO[eng]
            self.sem(auto)
            prev = self.cnt[auto]
            if prev > 0:
                wait.append((auto, prev))
            self.cnt[auto] += 1
            tgt = self.cnt[auto]
            assert inc is None or inc == auto, "compute ops inc their engine sem"
        elif inc is not None:
            self.sem(inc)
            self.cnt[inc] += inc_n
            tgt = self.cnt[inc]
        else:
            tgt = None

        def thunk(e, fn=fn, wait=tuple(wait), inc=inc, inc_n=inc_n, auto=auto):
            for s, v in wait:
                e.wait_ge(self.sems[s], v)
            ins = fn(e)
            if auto is not None:
                assert ins is not None, "compute op must return its instruction"
                ins.then_inc(self.sems[auto], 1)
            elif inc is not None:
                assert ins is not None
                ins.then_inc(self.sems[inc], inc_n)

        self.q[eng].append(thunk)
        return tgt

    def wait(self, eng, sems):
        for s, _ in sems:
            self.sem(s)

        def thunk(e, sems=tuple(sems)):
            for s, v in sems:
                e.wait_ge(self.sems[s], v)

        self.q[eng].append(thunk)

    def now(self, sem):
        self.sem(sem)
        return self.cnt[sem]

    def build_and_close(self):
        nc = self.nc
        with nc.Block() as block:
            @block.sync
            def _(e):
                for t in self.q["sync"]:
                    t(e)

            @block.gpsimd
            def _(e):
                for t in self.q["gpsimd"]:
                    t(e)

            @block.vector
            def _(e):
                for t in self.q["vector"]:
                    t(e)

            @block.scalar
            def _(e):
                for t in self.q["scalar"]:
                    t(e)

            @block.tensor
            def _(e):
                for t in self.q["tensor"]:
                    t(e)
        while self._scopes:
            self._scopes.pop().close()
        self.es.close()


class Bank:
    def __init__(self, t):
        self.t = t
        self.last_evac = None

    def __getitem__(self, k):
        return self.t[k]

    def guard(self):
        return [self.last_evac] if self.last_evac is not None else []


def new_ctx():
    nc = bass.Bass("TRN2", target_bir_lowering=False)
    return Ctx(nc)


def add_consts(ctx):
    nc = ctx.nc
    cpool = ctx.sb("cpool", [128, len(CONST_VALS)], F32)
    for j, v in enumerate(CONST_VALS):
        ctx.op("vector", lambda e, t=cpool, j=j, v=v: e.memset(t[:, j:j + 1], v))
    ident = ctx.sb("ident", [128, 128], BF16)
    ones_col = ctx.sb("ones_col", [128, 1], BF16)
    ones_row = ctx.sb("ones_row", [1, 128], BF16)
    id_dram = nc.inline_tensor(np.eye(128, dtype=BF), "ident_d").ap()
    ctx.op("sync", lambda e, d=ident, s=id_dram: e.dma_start(out=d[:], in_=s),
           inc="ld_c", inc_n=16, dma=True)
    ctx.op("vector", lambda e, t=ones_col: e.memset(t[:], 1.0))
    ctx.op("vector", lambda e, t=ones_row: e.memset(t[:], 1.0))
    ctx.consts_ld = ctx.now("ld_c")
    ctx.consts_dv = ctx.now("dv")
    ctx.cpool, ctx.ident, ctx.ones_col, ctx.ones_row = cpool, ident, ones_col, ones_row
    return cpool


def cc(ctx, j):
    return ctx.cpool[:, j:j + 1]



def emit_load_x(ctx, X, x_dram, n_rows=TPC):
    nt = n_rows // 128
    v = x_dram.rearrange("(t p) d -> p t d", p=128)
    ctx.op("sync", lambda e, X=X, s=v: e.dma_start(out=X[:, 0:nt, :], in_=s),
           inc="ld_x", inc_n=16, dma=True)
    return ctx.now("ld_x")


def emit_ln_stats(ctx, X, t, stat_m, stat_r, sq_scr, wait_v=(), wait_s=(),
                  eps_col=C_LNEPS):
    xt = X[:, t, :]
    sumx = ctx.sb if False else None
    ac = ctx.op("scalar",
                lambda e, x=xt, s=sq_scr[:], a=stat_r[:, t:t + 1]: e.activation(
                    out=s, in_=x, func=AF.Square, accum_out=a),
                wait=wait_s)
    dv = ctx.op("vector",
                lambda e, x=xt, o=stat_m[:, t:t + 1]: e.reduce_sum(out=o, in_=x, axis=AX),
                wait=wait_v)
    m = stat_m[:, t:t + 1]
    r = stat_r[:, t:t + 1]
    ctx.op("vector", lambda e, m=m: e.tensor_scalar_mul(out=m, in0=m, scalar1=1.0 / D),
           wait=[("dv", dv)])
    ctx.op("vector", lambda e, r=r: e.tensor_scalar_mul(out=r, in0=r, scalar1=1.0 / D),
           wait=[("ac", ac)])
    dv2 = ctx.op("vector", lambda e, m=m, r=r: e.scalar_tensor_tensor(
        out=r, in0=m, scalar=m, in1=r, op0=ALU.mult, op1=ALU.subtract))
    ac2 = ctx.op("scalar",
                 lambda e, r=r, b=cc(ctx, eps_col): e.activation(
                     out=r, in_=r, func=AF.Sqrt, scale=-1.0, bias=b),
                 wait=[("dv", dv2)])
    dv3 = ctx.op("vector",
                 lambda e, r=r: e.reciprocal(out=r, in_=r),
                 wait=[("ac", ac2)])
    return dv3


def emit_h_tile(ctx, X, t, stat_m, stat_r, h_bf, wait_v=()):
    return ctx.op("vector",
                  lambda e, x=X[:, t, :], m=stat_m[:, t:t + 1],
                         r=stat_r[:, t:t + 1], o=h_bf[:]: e.tensor_scalar(
                             out=o, in0=x, scalar1=m, scalar2=r,
                             op0=ALU.subtract, op1=ALU.mult),
                  wait=wait_v)


def emit_transpose_tile(ctx, src_bf, ht, t, pt_banks, wait_pe=(), src_chunks=DC,
                        extra_dv_wait=()):
    dv_last = None
    n_half = (src_chunks + 7) // 8
    for half in range(n_half):
        bank = pt_banks[half % 2]
        d0 = half * 8
        nd = min(8, src_chunks - d0)
        w = list(wait_pe) + [("ld_c", ctx.consts_ld)] + bank.guard()
        for j in range(nd):
            d = d0 + j
            pe = ctx.op("tensor",
                        lambda e, s=src_bf[:, d * 128:(d + 1) * 128],
                               o=bank[:, j * 128:(j + 1) * 128], i=ctx.ident[:]:
                        e.transpose(o, s, i),
                        wait=w if j == 0 else ())
        dv_last = ctx.op("vector",
                         lambda e, s=bank[:, 0:nd * 128].rearrange("p (j c) -> p j c", c=128),
                                o=ht[:, d0:d0 + nd, t * 128:(t + 1) * 128]:
                         e.tensor_copy(out=o, in_=s),
                         wait=[("pe", pe)] + list(extra_dv_wait))
        bank.last_evac = ("dv", dv_last)
    return dv_last


def emit_floor_plus_half(ctx, u_ap, ianc, fanc, out_ap, eng="vector", wait=()):
    ctx.op(eng, lambda e, u=u_ap, i=ianc: e.tensor_copy(out=i, in_=u), wait=wait)
    ctx.op(eng, lambda e, i=ianc, f=fanc: e.tensor_single_scalar(
        out=f, in_=i, scalar=-1.0, op=ALU.add))
    ctx.op(eng, lambda e, u=u_ap, f=fanc, o=out_ap: e.scalar_tensor_tensor(
        out=o, in0=f, scalar=1.0, in1=u, op0=ALU.add, op1=ALU.subtract))
    ctx.op(eng, lambda e, o=out_ap, f=fanc: e.scalar_tensor_tensor(
        out=f, in0=o, scalar=-0.5, in1=f, op0=ALU.is_le, op1=ALU.add))
    last = ctx.op(eng, lambda e, o=out_ap, f=fanc: e.scalar_tensor_tensor(
        out=o, in0=o, scalar=0.5, in1=f, op0=ALU.is_lt if False else ALU.is_gt, op1=ALU.add),
        inc="dv")
    return last


def emit_floor(ctx, u_ap, ianc, fanc, rr, out_ap, wait=()):
    ctx.op("vector", lambda e, u=u_ap, i=ianc: e.tensor_copy(out=i, in_=u), wait=wait)
    ctx.op("vector", lambda e, i=ianc, f=fanc: e.tensor_single_scalar(
        out=f, in_=i, scalar=-1.0, op=ALU.add))
    ctx.op("vector", lambda e, u=u_ap, f=fanc, o=rr: e.scalar_tensor_tensor(
        out=o, in0=f, scalar=1.0, in1=u, op0=ALU.add, op1=ALU.subtract))
    return ctx.op("vector", lambda e, o=out_ap, f=fanc, r=rr: e.scalar_tensor_tensor(
        out=o, in0=r, scalar=0.0, in1=f, op0=ALU.is_le, op1=ALU.add))



def build_qkv_launch(core, layer):
    ctx = new_ctx()
    nc = ctx.nc
    FQ = 6144 if layer == 0 else 2304
    x_in = nc.dram_tensor("x_in", [TPC, D], F32, kind="ExternalInput").ap()
    wt_in = nc.dram_tensor("wt_in", [D, FQ], F32, kind="ExternalInput").ap()
    if layer == 1:
        ns_in = nc.dram_tensor("ns_in", [1, D], F32, kind="ExternalInput").ap()
        ss_in = nc.dram_tensor("ss_in", [1, D], F32, kind="ExternalInput").ap()
    qkv_out = nc.dram_tensor("qkv_out", [FQ, TPC], BF16, kind="ExternalOutput").ap()

    add_consts(ctx)
    X = ctx.sb("X", [128, NT, D], F32)
    HT = ctx.sb("HT", [128, DC, TPC], BF16)
    stat_m = ctx.sb("stat_m", [128, NT], F32)
    stat_r = ctx.sb("stat_r", [128, NT], F32)
    sq_scr = ctx.sb("sq_scr", [128, D], F32)
    h_bf = [ctx.sb(f"h_bf{i}", [128, D], BF16) for i in range(2)]
    pt = [Bank(ctx.ps(f"pt{i}", [128, 1024], BF16)) for i in range(2)]
    FB = 768
    wblk = [ctx.sb(f"wblk{i}", [128, DC, FB], BF16) for i in range(2)]
    stg = [ctx.sb(f"stg{i}", [128, TPC], BF16) for i in range(2)]
    pq = [ctx.ps(f"pq{i}", [128, 512], F32) for i in range(4)]
    if layer == 1:
        csb = ctx.sb("csb", [128, D], F32)
        csrow = ctx.sb("csrow", [1, D], F32)
        ones_row_f = ctx.sb("ones_row_f", [1, 128], F32)
        pbc = ctx.ps("pbc", [128, 512], F32)
        h_f32 = ctx.sb("h_f32", [128, D], F32)
        rho_scr = ctx.sb("rho_scr", [128, 1], F32)

    ld0 = emit_load_x(ctx, X, x_in)
    ctx.wait("vector", [("ld_c", ctx.consts_ld)])

    if layer == 1:
        ctx.op("sync", lambda e: e.dma_start(out=csrow[:], in_=ns_in), inc="ld", inc_n=16)
        ctx.op("sync", lambda e: e.dma_start(out=sq_scr[0:1, :], in_=ss_in), inc="ld", inc_n=16)
        lw = ctx.now("ld")
        ctx.op("vector", lambda e: e.tensor_tensor(out=csrow[:], in0=csrow[:], in1=sq_scr[0:1, :], op=ALU.mult), wait=[("ld", lw)], inc="dv")
        ctx.op("vector", lambda e: e.memset(ones_row_f[:], 1.0), inc="dv")
        dvc = ctx.now("dv")
        prev_cp = 0
        for j in range(4):
            pe = ctx.op("tensor", lambda e, j=j: e.matmul(
                pbc[:], ones_row_f[:], csrow[:, j * 512:(j + 1) * 512],
                start=True, stop=True),
                wait=[("dv", dvc), ("ac", prev_cp)])
            prev_cp = ctx.op("scalar", lambda e, j=j: e.copy(
                out=csb[:, j * 512:(j + 1) * 512], in_=pbc[:]), wait=[("pe", pe)])
        cs_ac = ctx.now("ac")

    for t in range(NT):
        emit_ln_stats(ctx, X, t, stat_m, stat_r, sq_scr,
                      wait_v=[("ld_x", ld0)], wait_s=[("ld_x", ld0)])
        hb = h_bf[t % 2]
        if layer == 0:
            hdv = emit_h_tile(ctx, X, t, stat_m, stat_r, hb)
        else:
            m = stat_m[:, t:t + 1]
            r = stat_r[:, t:t + 1]
            sc = rho_scr[:]
            ctx.op("vector", lambda e, r=r, sc=sc: e.tensor_tensor(
                out=sc, in0=r, in1=r, op=ALU.mult))
            dvp = ctx.op("vector", lambda e, sc=sc: e.tensor_scalar(
                out=sc, in0=sc, scalar1=-1e-5, scalar2=1.0 + 1e-8,
                op0=ALU.mult, op1=ALU.add))
            acs = ctx.op("scalar", lambda e, sc=sc: e.activation(
                out=sc, in_=sc, func=AF.Sqrt), wait=[("dv", dvp)])
            ctx.op("vector", lambda e, sc=sc: e.reciprocal(out=sc, in_=sc),
                   wait=[("ac", acs)])
            ctx.op("vector", lambda e, sc=sc, r=r: e.tensor_tensor(
                out=sc, in0=sc, in1=r, op=ALU.mult))
            ctx.op("vector", lambda e, t=t, m=m, sc=sc: e.tensor_scalar(
                out=h_f32[:], in0=X[:, t, :], scalar1=m, scalar2=sc,
                op0=ALU.subtract, op1=ALU.mult))
            hdv = ctx.op("vector", lambda e, hb=hb: e.tensor_tensor(
                out=hb[:], in0=h_f32[:], in1=csb[:], op=ALU.mult),
                wait=[("ac", cs_ac)])
        emit_transpose_tile(ctx, hb[:], HT, t, pt, wait_pe=[("dv", hdv)])
    ht_dv = ctx.now("dv")

    nfb = FQ // FB if FQ % FB == 0 else None
    if nfb is None:
        nfb = FQ // 768
    wsrc = wt_in.rearrange("(c p) f -> p c f", p=128)
    last_store = {}
    for fb in range(nfb):
        wb = wblk[fb % 2]
        wsem = f"wl{fb % 2}"
        wld = ctx.op("gpsimd", lambda e, fb=fb, wb=wb: e.dma_start(
            out=wb[:], in_=wsrc[:, :, fb * FB:(fb + 1) * FB]),
            wait=[("pe", last_store.get(fb % 2, 0))], inc=wsem, inc_n=16, dma=True)
        for fc in range(FB // 128):
            f_abs = fb * FB + fc * 128
            bk = [pq[(fc * 2) % 4], pq[(fc * 2) % 4 + 1]]
            for d in range(DC):
                for qb in range(2):
                    pe = ctx.op("tensor", lambda e, wb=wb, d=d, fc=fc, qb=qb, bk=bk: e.matmul(
                        bk[qb][:], wb[:, d, fc * 128:(fc + 1) * 128],
                        HT[:, d, qb * 512:(qb + 1) * 512],
                        start=(d == 0), stop=(d == DC - 1)),
                        wait=([(wsem, wld), ("dv", ht_dv),
                               ("ac", last_store.get(("b", (fc * 2) % 4), 0))]
                              if d == 0 and qb == 0 else ()))
            sg = stg[fc % 2]
            scale = 0.125 if f_abs < 2048 else 1.0
            acv = ctx.op("scalar", lambda e, sg=sg, bk=bk, scale=scale: e.activation(
                out=sg[:, 0:512], in_=bk[0][:], func=AF.Copy, scale=scale),
                wait=[("pe", pe), (f"st{fc % 2}", last_store.get(("s", fc % 2), 0))])
            acv = ctx.op("scalar", lambda e, sg=sg, bk=bk, scale=scale: e.activation(
                out=sg[:, 512:1024], in_=bk[1][:], func=AF.Copy, scale=scale))
            ssem = f"st{fc % 2}"
            stc = ctx.op("sync", lambda e, sg=sg, f_abs=f_abs: e.dma_start(
                out=qkv_out[f_abs:f_abs + 128, :], in_=sg[:]),
                wait=[("ac", acv)], inc=ssem, inc_n=16, dma=True)
            last_store[("s", fc % 2)] = stc
            last_store[("b", (fc * 2) % 4)] = acv
        last_store[fb % 2] = ctx.now("pe")
    ctx.wait("sync", [("st0", ctx.now("st0")), ("st1", ctx.now("st1"))])
    ctx.build_and_close()
    return ctx.nc


def qb_base(core, qb):
    return qtile_bases(core)[qb * 4]


def kt_lists(core, qb):
    b0 = qb_base(core, qb)
    full = list(range(0, b0 // 128))
    diag = [b0 // 128 + i for i in range(4)]
    return full, diag


def build_attn0(core):
    ctx = new_ctx()
    nc = ctx.nc
    x_in = nc.dram_tensor("x_in", [TPC, D], F32, kind="ExternalInput").ap()
    qf_in = nc.dram_tensor("qf_in", [2048, TPC], BF16, kind="ExternalInput").ap()
    kf_in = nc.dram_tensor("kf_in", [2048, S], BF16, kind="ExternalInput").ap()
    vf_in = nc.dram_tensor("vf_in", [2048, S], BF16, kind="ExternalInput").ap()
    lam_in = nc.dram_tensor("lam_in", [4, 128], F32, kind="ExternalInput").ap()
    gnt_in = nc.dram_tensor("gnt_in", [128, 32], F32, kind="ExternalInput").ap()
    wot_in = nc.dram_tensor("wot_in", [D, D], F32, kind="ExternalInput").ap()
    x1_out = nc.dram_tensor("x1_out", [TPC, D], F32, kind="ExternalOutput").ap()

    add_consts(ctx)
    onescol_f = ctx.sb("onescol_f", [128, 1], F32)
    onesrow_f = ctx.sb("onesrow_f", [1, 128], F32)
    ctx.op("vector", lambda e: e.memset(onescol_f[:], 1.0))
    ctx.op("vector", lambda e: e.memset(onesrow_f[:], 1.0))

    X = ctx.sb("X", [128, NT, D], F32)
    ld_x = emit_load_x(ctx, X, x_in)

    lamt = ctx.sb("lamt", [128, 6], F32)
    for i in range(4):
        ctx.op("sync", lambda e, i=i: e.dma_start(
            out=lamt[:, i:i + 1], in_=lam_in[i, :, None]), inc="ld_lam", inc_n=16, dma=True)
    gnt = ctx.sb("gnt", [128, 32], F32)
    ctx.op("sync", lambda e: e.dma_start(out=gnt[:], in_=gnt_in),
           inc="ld_lam", inc_n=16, dma=True)
    llam = ctx.now("ld_lam")
    ctx.op("vector", lambda e: e.tensor_tensor(
        out=lamt[:, 4:5], in0=lamt[:, 0:1], in1=lamt[:, 1:2], op=ALU.mult),
        wait=[("ld_lam", llam)])
    ctx.op("vector", lambda e: e.tensor_tensor(
        out=lamt[:, 5:6], in0=lamt[:, 2:3], in1=lamt[:, 3:4], op=ALU.mult))
    ctx.op("scalar", lambda e: e.activation(out=lamt[:, 4:5], in_=lamt[:, 4:5], func=AF.Exp),
           wait=[("dv", ctx.now("dv"))])
    a_e2 = ctx.op("scalar", lambda e: e.activation(out=lamt[:, 5:6], in_=lamt[:, 5:6], func=AF.Exp))
    ctx.op("vector", lambda e: e.tensor_tensor(
        out=lamt[:, 4:5], in0=lamt[:, 4:5], in1=lamt[:, 5:6], op=ALU.subtract),
        wait=[("ac", a_e2)])
    ctx.op("vector", lambda e: e.tensor_scalar_add(
        out=lamt[:, 4:5], in0=lamt[:, 4:5], scalar1=LAMBDA_INIT))
    dv_lam = ctx.op("vector", lambda e: e.tensor_scalar_mul(
        out=lamt[:, 5:6], in0=lamt[:, 4:5], scalar1=-1.0))
    ac_gn = ctx.op("scalar", lambda e: e.activation(
        out=gnt[:], in_=gnt[:], func=AF.Copy, scale=1.0 - LAMBDA_INIT),
        wait=[("ld_lam", llam)])

    WoIn = ctx.sb("WoIn", [128, 16, TPC], BF16)
    ctx.push_scope()
    q1a = ctx.sb("q1a", [65, TPC], BF16)
    q2a = ctx.sb("q2a", [65, TPC], BF16)
    k1a = ctx.sb("k1a", [65, S], BF16)
    k2a = ctx.sb("k2a", [65, S], BF16)
    vh = ctx.sb("vh", [128, S], BF16)
    vtok = ctx.sb("vtok", [128, 16, 128], BF16)
    PT1 = ctx.sb("PT1", [128, 16, 512], BF16)
    PT2 = ctx.sb("PT2", [128, 16, 512], BF16)
    ksq = ctx.sb("ksq", [64, S], BF16)
    norm_sc = ctx.sb("norm_sc", [1, 2048], F32)
    qn_sc = ctx.sb("qn_sc", [1, TPC], F32)
    c_sc = ctx.sb("c_sc", [1, 2], F32)
    rs_sc = ctx.sb("rs_sc", [1, 1024], F32)
    mrow = ctx.sb("mrow", [1, 512], F32)
    vrow = ctx.sb("vrow", [1, 512], F32)
    bc = ctx.sb("bc", [128, 4, 1024], F32)
    t1s = ctx.sb("t1s", [128, 512], F32)
    t2s = ctx.sb("t2s", [128, 512], F32)
    Acomb = ctx.sb("Acomb", [128, TPC], F32)
    Asq = ctx.sb("Asq", [128, TPC], F32)

    ps_sc = [Bank(ctx.ps("ps_sc0", [128, 512], F32)), Bank(ctx.ps("ps_sc1", [128, 512], F32))]
    ps_av1 = Bank(ctx.ps("ps_av1", [128, 512], F32))
    ps_av2 = Bank(ctx.ps("ps_av2", [128, 512], F32))
    ps_sum1 = Bank(ctx.ps("ps_sum1", [1, 512], F32))
    ps_sum2 = Bank(ctx.ps("ps_sum2", [1, 512], F32))
    pt = [Bank(ctx.ps("pt0", [128, 1024], BF16)), Bank(ctx.ps("pt1", [128, 1024], BF16))]

    ctx.op("vector", lambda e: e.memset(k1a[64:65, :], 1.0))
    ctx.op("vector", lambda e: e.memset(k2a[64:65, :], 1.0))
    dv_ones = ctx.now("dv")

    bases = qtile_bases(core)
    last_head_pe = 0
    last_vtok_pe = 0
    prev_gn = {}

    for h in range(16):
        lh = f"ldh{h % 2}"
        w_guard = [("pe", last_head_pe)]
        ctx.op("sync", lambda e, h=h: e.dma_start(
            out=q1a[0:64, :], in_=qf_in[h * 128:h * 128 + 64, :]),
            wait=w_guard, inc=lh, inc_n=16, dma=True)
        ctx.op("sync", lambda e, h=h: e.dma_start(
            out=q2a[0:64, :], in_=qf_in[h * 128 + 64:h * 128 + 128, :]),
            inc=lh, inc_n=16, dma=True)
        ctx.op("sync", lambda e, h=h: e.dma_start(
            out=k1a[0:64, :], in_=kf_in[h * 128:h * 128 + 64, :]),
            inc=lh, inc_n=16, dma=True)
        ctx.op("sync", lambda e, h=h: e.dma_start(
            out=k2a[0:64, :], in_=kf_in[h * 128 + 64:h * 128 + 128, :]),
            inc=lh, inc_n=16, dma=True)
        ctx.op("sync", lambda e, h=h: e.dma_start(
            out=vh[:], in_=vf_in[h * 128:h * 128 + 128, :]),
            inc=lh, inc_n=16, dma=True)
        ldh = ctx.now(lh)

        dv_vt = emit_transpose_tile(ctx, vh[:], vtok, 0, pt,
                                    wait_pe=[(lh, ldh)], src_chunks=16,
                                    extra_dv_wait=[("pe", last_head_pe)])

        for br, (qa, ka) in enumerate([(q1a, k1a), (q2a, k2a)]):
            acq = ctx.op("scalar", lambda e, ka=ka: e.activation(
                out=ksq[:], in_=ka[0:64, :], func=AF.Square),
                wait=[(lh, ldh), ("pe", prev_gn.get("ksq_pe", 0))])
            for blk in range(4):
                pe = ctx.op("tensor", lambda e, blk=blk: e.matmul(
                    ps_sum1[:], ctx.ones_col[0:64, :], ksq[:, blk * 512:(blk + 1) * 512],
                    start=True, stop=True),
                    wait=[("ac", acq), ("dv", prev_gn.get("sum1_dv", 0))])
                prev_gn["sum1_dv"] = ctx.op("vector", lambda e, blk=blk: e.tensor_copy(
                    out=norm_sc[:, blk * 512:(blk + 1) * 512], in_=ps_sum1[:]),
                    wait=[("pe", pe)])
            prev_gn["ksq_pe"] = pe
            ctx.op("vector", lambda e: e.reduce_max(
                out=c_sc[:, 0:1], in_=norm_sc[:], axis=AX))
            acc = ctx.op("scalar", lambda e: e.activation(
                out=c_sc[:, 0:1], in_=c_sc[:, 0:1], func=AF.Sqrt),
                wait=[("dv", ctx.now("dv"))])
            dvc = ctx.op("vector", lambda e: e.tensor_scalar_mul(
                out=c_sc[:, 1:2], in0=c_sc[:, 0:1], scalar1=-1.0),
                wait=[("ac", acc)])
            acq2 = ctx.op("scalar", lambda e, qa=qa: e.activation(
                out=ksq[:, 0:TPC], in_=qa[0:64, :], func=AF.Square))
            for blk in range(2):
                pe = ctx.op("tensor", lambda e, blk=blk: e.matmul(
                    ps_sum1[:], ctx.ones_col[0:64, :], ksq[:, blk * 512:(blk + 1) * 512],
                    start=True, stop=True),
                    wait=[("ac", acq2), ("dv", prev_gn.get("sum1_dv", 0))])
                prev_gn["sum1_dv"] = ctx.op("vector", lambda e, blk=blk: e.tensor_copy(
                    out=qn_sc[:, blk * 512:(blk + 1) * 512], in_=ps_sum1[:]),
                    wait=[("pe", pe)])
            prev_gn["ksq_pe"] = pe
            acq3 = ctx.op("scalar", lambda e: e.activation(
                out=qn_sc[:], in_=qn_sc[:], func=AF.Sqrt),
                wait=[("dv", prev_gn["sum1_dv"])])
            dvb = ctx.op("vector", lambda e, qa=qa: e.tensor_single_scalar(
                out=qa[64:65, :], in_=qn_sc[:], scalar=c_sc[:, 1:2], op=ALU.mult),
                wait=[("ac", acq3), ("dv", dvc)])
            prev_gn[f"brow{br}"] = dvb

        for qb in range(2):
            full, diag = kt_lists(core, qb)
            kts = full + diag
            b0 = qb_base(core, qb)
            for br, (qa, ka, PT, ps_sum, ps_av) in enumerate(
                    [(q1a, k1a, PT1, ps_sum1, ps_av1),
                     (q2a, k2a, PT2, ps_sum2, ps_av2)]):
                for i, kt in enumerate(kts):
                    bank = ps_sc[i % 2]
                    w = bank.guard() + [("dv", dv_ones), ("dv", prev_gn[f"brow{br}"])]
                    if i == 0:
                        w.append(("po", prev_gn.get(f"pt_mask{br}", 0)))
                    pe = ctx.op("tensor", lambda e, bank=bank, ka=ka, qa=qa, kt=kt, qb=qb:
                                e.matmul(bank[:], ka[:, kt * 128:(kt + 1) * 128],
                                         qa[:, qb * 512:(qb + 1) * 512],
                                         start=True, stop=True), wait=w)
                    bank.last_evac = ("ac", ctx.op("scalar", lambda e, bank=bank, PT=PT, kt=kt:
                                            e.activation(out=PT[:, kt, :], in_=bank[:],
                                                         func=AF.Exp),
                                            wait=[("pe", pe)]))
                ac_exp = ctx.now("ac")
                for kt in diag:
                    prev_gn[f"pt_mask{br}"] = ctx.op("gpsimd",
                        lambda e, PT=PT, kt=kt, b0=b0: e.affine_select(
                            out=PT[:, kt, :], in_=PT[:, kt, :], pattern=[[1, 512]],
                            compare_op=ALU.is_ge, fill=0.0,
                            base=b0 - kt * 128, channel_multiplier=-1),
                        wait=[("ac", ac_exp)])
                po_mask = ctx.now("po")
                for i, kt in enumerate(kts):
                    w = [("po", po_mask)] if i == 0 else ()
                    if i == 0:
                        w = list(w) + [("dv", prev_gn.get(f"sums_used{br}", 0))]
                    ctx.op("tensor", lambda e, PT=PT, kt=kt, i=i, ps_sum=ps_sum, n=len(kts):
                           e.matmul(ps_sum[:], ctx.ones_col[:], PT[:, kt, :],
                                    start=(i == 0), stop=(i == n - 1)), wait=w)
                for i, kt in enumerate(kts):
                    w = []
                    if i == 0:
                        w = [("dv", prev_gn.get(f"av_used{br}", 0)), ("dv", dv_vt)]
                    pe_av = ctx.op("tensor", lambda e, PT=PT, kt=kt, i=i, ps_av=ps_av, n=len(kts):
                                   e.matmul(ps_av[:], vtok[:, kt, :], PT[:, kt, :],
                                            start=(i == 0), stop=(i == n - 1)),
                                   wait=w)
                prev_gn[f"avpe{br}"] = pe_av
            last_head_pe = ctx.now("pe")
            dvr1 = ctx.op("vector", lambda e: e.reciprocal(
                out=rs_sc[:, 0:512], in_=ps_sum1[:]),
                wait=[("pe", ctx.now("pe")), ("ac", ctx.now("ac")), ("po", ctx.now("po"))])
            dvr2 = ctx.op("vector", lambda e: e.reciprocal(
                out=rs_sc[:, 512:1024], in_=ps_sum2[:]))
            prev_gn["sums_used0"] = prev_gn["sums_used1"] = dvr2
            for br, dvr in ((0, dvr1), (1, dvr2)):
                pe = ctx.op("tensor", lambda e, br=br: e.matmul(
                    ps_sc[0][:], onesrow_f[:], rs_sc[:, br * 512:(br + 1) * 512],
                    start=True, stop=True),
                    wait=[("dv", dvr), ("dv", prev_gn.get("bc_used", 0))] + ps_sc[0].guard())
                ps_sc[0].last_evac = ("ac", ctx.op("scalar", lambda e, br=br, qb=qb: e.copy(
                    out=bc[:, br, qb * 512:(qb + 1) * 512], in_=ps_sc[0][:]),
                    wait=[("pe", pe)]))
            ac_bc = ctx.now("ac")
            dv_t1 = ctx.op("vector", lambda e, qb=qb: e.tensor_tensor(
                out=t1s[:], in0=ps_av1[:], in1=bc[:, 0, qb * 512:(qb + 1) * 512],
                op=ALU.mult), wait=[("ac", ac_bc)])
            dv_t2 = ctx.op("vector", lambda e, qb=qb: e.tensor_tensor(
                out=t2s[:], in0=ps_av2[:], in1=bc[:, 1, qb * 512:(qb + 1) * 512],
                op=ALU.mult))
            prev_gn["av_used0"] = prev_gn["av_used1"] = dv_t2
            dv_ac = ctx.op("vector", lambda e, qb=qb: e.scalar_tensor_tensor(
                out=Acomb[:, qb * 512:(qb + 1) * 512], in0=t2s[:], scalar=lamt[:, 5:6],
                in1=t1s[:], op0=ALU.mult, op1=ALU.add),
                wait=[("dv", dv_lam)])
            ac_sq = ctx.op("scalar", lambda e, qb=qb: e.activation(
                out=Asq[:, qb * 512:(qb + 1) * 512], in_=Acomb[:, qb * 512:(qb + 1) * 512],
                func=AF.Square), wait=[("dv", dv_ac)])
            pe_s1 = ctx.op("tensor", lambda e, qb=qb: e.matmul(
                ps_sum1[:], onescol_f[:], Acomb[:, qb * 512:(qb + 1) * 512],
                start=True, stop=True), wait=[("dv", dv_ac), ("dv", dvr2)])
            pe_s2 = ctx.op("tensor", lambda e, qb=qb: e.matmul(
                ps_sum2[:], onescol_f[:], Asq[:, qb * 512:(qb + 1) * 512],
                start=True, stop=True), wait=[("ac", ac_sq)])
            dv_m = ctx.op("vector", lambda e: e.tensor_scalar_mul(
                out=mrow[:], in0=ps_sum1[:], scalar1=1.0 / 128), wait=[("pe", pe_s2)])
            ctx.op("vector", lambda e: e.tensor_tensor(
                out=vrow[:], in0=mrow[:], in1=mrow[:], op=ALU.mult))
            dv_v = ctx.op("vector", lambda e: e.scalar_tensor_tensor(
                out=vrow[:], in0=ps_sum2[:], scalar=1.0 / 128, in1=vrow[:],
                op0=ALU.mult, op1=ALU.subtract))
            prev_gn["sums_used0"] = prev_gn["sums_used1"] = dv_v
            prev_gn["sum1_dv"] = dv_v
            ac_sd = ctx.op("scalar", lambda e: e.activation(
                out=vrow[:], in_=vrow[:], func=AF.Sqrt, bias=cc(ctx, C_GNEPS)[0:1, :]),
                wait=[("dv", dv_v)])
            dv_rsd = ctx.op("vector", lambda e: e.reciprocal(out=vrow[:], in_=vrow[:]),
                            wait=[("ac", ac_sd)])
            for slot, row, dvw in ((2, mrow, dv_m), (3, vrow, dv_rsd)):
                pe = ctx.op("tensor", lambda e, row=row: e.matmul(
                    ps_sc[0][:], onesrow_f[:], row[:], start=True, stop=True),
                    wait=[("dv", dvw)] + ps_sc[0].guard())
                ps_sc[0].last_evac = ("ac", ctx.op("scalar", lambda e, slot=slot, qb=qb: e.copy(
                    out=bc[:, slot, qb * 512:(qb + 1) * 512], in_=ps_sc[0][:]),
                    wait=[("pe", pe)]))
            ac_bc2 = ctx.now("ac")
            dvu = ctx.op("vector", lambda e, qb=qb: e.tensor_tensor(
                out=t1s[:], in0=Acomb[:, qb * 512:(qb + 1) * 512],
                in1=bc[:, 2, qb * 512:(qb + 1) * 512], op=ALU.subtract),
                wait=[("ac", ac_bc2)])
            dvu = ctx.op("vector", lambda e, qb=qb: e.tensor_tensor(
                out=t1s[:], in0=t1s[:], in1=bc[:, 3, qb * 512:(qb + 1) * 512],
                op=ALU.mult))
            dvw = ctx.op("vector", lambda e, h=h, qb=qb: e.tensor_scalar(
                out=WoIn[:, h, qb * 512:(qb + 1) * 512], in0=t1s[:],
                scalar1=gnt[:, h:h + 1], scalar2=gnt[:, 16 + h:17 + h],
                op0=ALU.mult, op1=ALU.add), wait=[("ac", ac_gn)])
            prev_gn["bc_used"] = dvw
    dv_woin = ctx.now("dv")

    ctx.pop_scope()
    reuse_bar = ctx.barrier_wait()
    wob = [ctx.sb(f"wob{i}", [128, DC, 512], BF16) for i in range(2)]
    osb = ctx.sb("osb", [128, 16, TPC], BF16)
    wsrc = wot_in.rearrange("(c p) f -> p c f", p=128)
    pw = [ps_av1, ps_av2]
    last_wo_pe = {}
    for ob in range(4):
        wb = wob[ob % 2]
        wld = ctx.op("gpsimd", lambda e, ob=ob, wb=wb: e.dma_start(
            out=wb[:], in_=wsrc[:, :, ob * 512:(ob + 1) * 512]),
            wait=list(reuse_bar) + [("pe", last_wo_pe.get(ob % 2, 0))],
            inc=f"wl{ob % 2}", inc_n=16, dma=True)
        wldc = ctx.now(f"wl{ob % 2}")
        for fc in range(4):
            for qbh in range(2):
                bank = pw[(fc * 2 + qbh) % 2]
                w = list(bank.guard())
                if fc == 0 and qbh == 0:
                    w += [(f"wl{ob % 2}", wldc), ("dv", dv_woin),
                          ("pe", prev_gn.get("avpe1", 0))]
                for d in range(DC):
                    pe = ctx.op("tensor", lambda e, wb=wb, d=d, fc=fc, qbh=qbh, bank=bank:
                                e.matmul(bank[:], wb[:, d, fc * 128:(fc + 1) * 128],
                                         WoIn[:, d, qbh * 512:(qbh + 1) * 512],
                                         start=(d == 0), stop=(d == DC - 1)),
                                wait=w if d == 0 else ())
                bank.last_evac = ("ac", ctx.op("scalar",
                    lambda e, bank=bank, ob=ob, fc=fc, qbh=qbh: e.copy(
                        out=osb[:, ob * 4 + fc, qbh * 512:(qbh + 1) * 512], in_=bank[:]),
                    wait=[("pe", pe)]))
        last_wo_pe[ob % 2] = ctx.now("pe")
    ac_osb = ctx.now("ac")

    for t in range(NT):
        for g in range(2):
            bank = pt[g % 2]
            w = [("ac", ac_osb), ("ld_x", ld_x)] + bank.guard()
            for j in range(8):
                d = g * 8 + j
                pe = ctx.op("tensor", lambda e, d=d, j=j, t=t, bank=bank: e.transpose(
                    bank[:, j * 128:(j + 1) * 128], osb[:, d, t * 128:(t + 1) * 128],
                    ctx.ident[:]), wait=w if j == 0 else ())
            bank.last_evac = ("dv", ctx.op("vector", lambda e, t=t, g=g, bank=bank: e.tensor_tensor(
                out=X[:, t, g * 1024:(g + 1) * 1024], in0=X[:, t, g * 1024:(g + 1) * 1024],
                in1=bank[:], op=ALU.add), wait=[("pe", pe)]))
        stv = ctx.op("sync", lambda e, t=t: e.dma_start(
            out=x1_out.rearrange("(t p) d -> p t d", p=128)[:, t, :], in_=X[:, t, :]),
            wait=[("dv", ctx.now("dv"))], inc="st_out", inc_n=16, dma=True)
    ctx.wait("sync", [("st_out", ctx.now("st_out"))])
    ctx.build_and_close()
    return ctx.nc


def build_ffn0(core):
    ctx = new_ctx()
    nc = ctx.nc
    H = 8192
    x_in = nc.dram_tensor("x_in", [TPC, D], F32, kind="ExternalInput").ap()
    wgt_in = nc.dram_tensor("wgt_in", [D, H], F32, kind="ExternalInput").ap()
    wut_in = nc.dram_tensor("wut_in", [D, H], F32, kind="ExternalInput").ap()
    wdt_in = nc.dram_tensor("wdt_in", [H, D], F32, kind="ExternalInput").ap()
    x2_out = nc.dram_tensor("x2_out", [TPC, D], F32, kind="ExternalOutput").ap()
    pT_dram = nc.dram_tensor("pT_dram", [H, TPC], BF16, kind="Internal").ap()

    add_consts(ctx)
    X = ctx.sb("X", [128, NT, D], F32)
    HT = ctx.sb("HT", [128, DC, TPC], BF16)
    stat_m = ctx.sb("stat_m", [128, NT], F32)
    stat_r = ctx.sb("stat_r", [128, NT], F32)
    rs_arr = ctx.sb("rs_arr", [128, NT], F32)
    s_arr = ctx.sb("s_arr", [128, NT], F32)
    sq_scr = ctx.sb("sq_scr", [128, D], F32)
    ld_x = emit_load_x(ctx, X, x_in)

    ctx.push_scope()
    xq = ctx.sb("xq", [128, D], F32)
    qa = ctx.sb("qa", [128, D], F32)
    ian = ctx.sb("ian", [128, D], I32)
    fan = ctx.sb("fan", [128, D], F32)
    rrb = ctx.sb("rrb", [128, D], F32)
    sg = ctx.sb("sg", [128, D], F32)
    kqb = ctx.sb("kqb", [128, D], BF16)
    rho = ctx.sb("rho", [128, 1], F32)
    pt = [Bank(ctx.ps("pt0", [128, 1024], BF16)), Bank(ctx.ps("pt1", [128, 1024], BF16))]
    ctx.wait("vector", [("ld_c", ctx.consts_ld)])
    for t in range(NT):
        emit_ln_stats(ctx, X, t, stat_m, stat_r, sq_scr,
                      wait_v=[("ld_x", ld_x)], wait_s=[("ld_x", ld_x)])
        m = stat_m[:, t:t + 1]
        r = stat_r[:, t:t + 1]
        ctx.op("vector", lambda e, r=r: e.tensor_tensor(out=rho[:], in0=r, in1=r, op=ALU.mult))
        dvp = ctx.op("vector", lambda e: e.tensor_scalar(
            out=rho[:], in0=rho[:], scalar1=-1e-5, scalar2=1.0 + 1e-8,
            op0=ALU.mult, op1=ALU.add))
        acs = ctx.op("scalar", lambda e: e.activation(out=rho[:], in_=rho[:], func=AF.Sqrt),
                     wait=[("dv", dvp)])
        ctx.op("vector", lambda e: e.reciprocal(out=rho[:], in_=rho[:]), wait=[("ac", acs)])
        ctx.op("vector", lambda e, r=r: e.tensor_tensor(out=rho[:], in0=rho[:], in1=r, op=ALU.mult))
        dvx = ctx.op("vector", lambda e, t=t, m=m: e.tensor_scalar(
            out=xq[:], in0=X[:, t, :], scalar1=m, scalar2=rho[:],
            op0=ALU.subtract, op1=ALU.mult))
        dvm = ctx.op("vector", lambda e, t=t: e.reduce_max(
            out=s_arr[:, t:t + 1], in_=xq[:], axis=AX, apply_absolute_value=True))
        ctx.op("vector", lambda e, t=t: e.tensor_scalar_mul(
            out=rs_arr[:, t:t + 1], in0=s_arr[:, t:t + 1], scalar1=1.0 / 127.0))
        ctx.op("vector", lambda e, t=t: e.reciprocal(
            out=s_arr[:, t:t + 1], in_=s_arr[:, t:t + 1]))
        dvs = ctx.op("vector", lambda e, t=t: e.tensor_scalar_mul(
            out=s_arr[:, t:t + 1], in0=s_arr[:, t:t + 1], scalar1=127.0))
        ac_sg = ctx.op("scalar", lambda e: e.sign(out=sg[:], in_=xq[:]), wait=[("dv", dvx)])
        ac_ab = ctx.op("scalar", lambda e: e.activation(out=qa[:], in_=xq[:], func=AF.Abs))
        dvu = ctx.op("vector", lambda e, t=t: e.tensor_scalar(
            out=qa[:], in0=qa[:], scalar1=s_arr[:, t:t + 1], scalar2=0.5,
            op0=ALU.mult, op1=ALU.add), wait=[("ac", ac_ab), ("dv", dvs)])
        emit_floor(ctx, qa[:], ian[:], fan[:], rrb[:], xq[:])
        dvk = ctx.op("vector", lambda e: e.scalar_tensor_tensor(
            out=kqb[:], in0=xq[:], scalar=127.0, in1=sg[:], op0=ALU.min, op1=ALU.mult),
            wait=[("ac", ac_sg)])
        emit_transpose_tile(ctx, kqb[:], HT, t, pt, wait_pe=[("dv", dvk)])
    dv_kqt = ctx.now("dv")
    ctx.pop_scope()
    barA = ctx.barrier_wait()

    ctx.push_scope()
    wgb = ctx.sb("wgb", [128, DC, 512], BF16)
    wub = ctx.sb("wub", [128, DC, 512], BF16)
    gsb = ctx.sb("gsb", [128, 512], F32)
    usb = ctx.sb("usb", [128, 512], F32)
    pb = ctx.sb("pb", [128, 512], BF16)
    ptst = [ctx.sb(f"ptst{i}", [128, 512], BF16) for i in range(2)]
    psg = Bank(ctx.ps("psg", [128, 512], F32))
    psu = Bank(ctx.ps("psu", [128, 512], F32))
    ptB = [Bank(ctx.ps("ptB0", [128, 1024], BF16)), Bank(ctx.ps("ptB1", [128, 1024], BF16))]
    wg_src = wgt_in.rearrange("(c p) f -> p c f", p=128)
    wu_src = wut_in.rearrange("(c p) f -> p c f", p=128)
    last_mm = 0
    st_cnt = {}
    for ob in range(16):
        lw = f"wl{ob % 2}"
        w0 = list(barA) if ob == 0 else [("pe", last_mm)]
        ctx.op("gpsimd", lambda e, ob=ob: e.dma_start(
            out=wgb[:], in_=wg_src[:, :, ob * 512:(ob + 1) * 512]),
            wait=w0, inc=lw, inc_n=16, dma=True)
        ctx.op("gpsimd", lambda e, ob=ob: e.dma_start(
            out=wub[:], in_=wu_src[:, :, ob * 512:(ob + 1) * 512]),
            inc=lw, inc_n=16, dma=True)
        lwc = ctx.now(lw)
        acg = ctx.op("scalar", lambda e: e.sign(
            out=wgb[:].rearrange("p c f -> p (c f)"), in_=wgb[:].rearrange("p c f -> p (c f)")),
            wait=[(lw, lwc)])
        acu = ctx.op("scalar", lambda e: e.sign(
            out=wub[:].rearrange("p c f -> p (c f)"), in_=wub[:].rearrange("p c f -> p (c f)")))
        for t in range(NT):
            wmm = [("ac", acu), ("dv", dv_kqt)] + psg.guard() + psu.guard()
            for d in range(DC):
                ctx.op("tensor", lambda e, d=d, t=t: e.matmul(
                    psg[:], HT[:, d, t * 128:(t + 1) * 128], wgb[:, d, :],
                    start=(d == 0), stop=(d == DC - 1)),
                    wait=wmm if d == 0 else ())
                pe_mm = ctx.op("tensor", lambda e, d=d, t=t: e.matmul(
                    psu[:], HT[:, d, t * 128:(t + 1) * 128], wub[:, d, :],
                    start=(d == 0), stop=(d == DC - 1)))
            psg.last_evac = ("ac", ctx.op("scalar", lambda e, t=t: e.activation(
                out=gsb[:], in_=psg[:], func=AF.Silu, scale=rs_arr[:, t:t + 1]),
                wait=[("pe", pe_mm)]))
            psu.last_evac = ("dv", ctx.op("vector", lambda e, t=t: e.tensor_single_scalar(
                out=usb[:], in_=psu[:], scalar=rs_arr[:, t:t + 1], op=ALU.mult),
                wait=[("pe", pe_mm)]))
            dv_p = ctx.op("vector", lambda e: e.tensor_tensor(
                out=pb[:], in0=gsb[:], in1=usb[:], op=ALU.mult),
                wait=[psg.last_evac])
            bankB = ptB[t % 2]
            stg = ptst[t % 2]
            wtr = [("dv", dv_p)] + bankB.guard()
            for j in range(4):
                pe_tr = ctx.op("tensor", lambda e, j=j, bankB=bankB: e.transpose(
                    bankB[:, j * 128:(j + 1) * 128], pb[:, j * 128:(j + 1) * 128],
                    ctx.ident[:]), wait=wtr if j == 0 else ())
            bankB.last_evac = ("dv", ctx.op("vector", lambda e, bankB=bankB, stg=stg: e.tensor_copy(
                out=stg[:], in_=bankB[:, 0:512]),
                wait=[("pe", pe_tr), (f"stp{t % 2}", st_cnt.get(t % 2, 0))]))
            for j in range(4):
                st_cnt[t % 2] = ctx.op("sync", lambda e, ob=ob, t=t, j=j, stg=stg: e.dma_start(
                    out=pT_dram[ob * 512 + j * 128:ob * 512 + (j + 1) * 128,
                                t * 128:(t + 1) * 128], in_=stg[:, j * 128:(j + 1) * 128]),
                    wait=[bankB.last_evac] if j == 0 else (),
                    inc=f"stp{t % 2}", inc_n=16, dma=True)
        last_mm = ctx.now("pe")
    pT_done = [("stp0", ctx.now("stp0")), ("stp1", ctx.now("stp1"))]
    ctx.pop_scope()
    barB = ctx.barrier_wait()

    ctx.push_scope()
    ptc = [ctx.sb(f"ptc{i}", [128, TPC], BF16) for i in range(2)]
    wdc = [ctx.sb(f"wdc{i}", [128, 512], BF16) for i in range(2)]
    psd = [Bank(ctx.ps(f"psd{i}", [128, 512], F32)) for i in range(NT)]
    wd_src = wdt_in.rearrange("(c p) f -> p c f", p=128)
    NF = H // 128
    ctx._ptc_use = {}
    for oq in range(4):
        for f in range(NF):
            lp = f"lp{f % 2}"
            lw = f"lw{f % 2}"
            first = (oq == 0 and f < 2)
            ctx.op("gpsimd", lambda e, f=f, oq=oq: e.dma_start(
                out=ptc[f % 2][:], in_=pT_dram[f * 128:(f + 1) * 128, :]),
                wait=(list(barB) + list(pT_done) if first else []) +
                     [("pe", ctx._ptc_use.get(f % 2, 0))],
                inc=lp, inc_n=16, dma=True)
            ctx.op("gpsimd", lambda e, f=f, oq=oq: e.dma_start(
                out=wdc[f % 2][:], in_=wd_src[:, f, oq * 512:(oq + 1) * 512]),
                inc=lw, inc_n=16, dma=True)
            lwc, lpc = ctx.now(lw), ctx.now(lp)
            ac_sd = ctx.op("scalar", lambda e, f=f: e.sign(
                out=wdc[f % 2][:], in_=wdc[f % 2][:]), wait=[(lw, lwc)])
            for t in range(NT):
                wmm = []
                if t == 0:
                    wmm = [(lp, lpc), ("ac", ac_sd)]
                    if f == 0:
                        wmm += psd[0].guard() + psd[NT - 1].guard()
                pe_d = ctx.op("tensor", lambda e, f=f, t=t, oq=oq: e.matmul(
                    psd[t][:], ptc[f % 2][:, t * 128:(t + 1) * 128], wdc[f % 2][:],
                    start=(f == 0), stop=(f == NF - 1)),
                    wait=wmm)
            ctx._ptc_use[f % 2] = ctx.now("pe")
        for t in range(NT):
            psd[t].last_evac = ("dv", ctx.op("vector", lambda e, t=t, oq=oq: e.tensor_tensor(
                out=X[:, t, oq * 512:(oq + 1) * 512], in0=X[:, t, oq * 512:(oq + 1) * 512],
                in1=psd[t][:], op=ALU.add), wait=[("pe", ctx.now("pe"))] if t == 0 else ()))
    for t in range(NT):
        ctx.op("sync", lambda e, t=t: e.dma_start(
            out=x2_out.rearrange("(t p) d -> p t d", p=128)[:, t, :], in_=X[:, t, :]),
            wait=[("dv", ctx.now("dv"))], inc="st_out", inc_n=16, dma=True)
    ctx.wait("sync", [("st_out", ctx.now("st_out"))])
    ctx.build_and_close()
    return ctx.nc


def emit_fp8_quant(ctx, w_ap, out_ap, qs, wait=()):
    a, ian, fan, rr, ef, ui, t1 = (qs[k] for k in ("a", "ian", "fan", "rr", "ef", "ui", "t1"))
    P_ = w_ap.shape[0]
    ac1 = ctx.op("scalar", lambda e: e.activation(out=a, in_=w_ap, func=AF.Abs), wait=wait)
    ac2 = ctx.op("scalar", lambda e: e.activation(out=a, in_=a, func=AF.Ln,
                                                  bias=cc(ctx, 8)[0:P_, :]))
    dv1 = ctx.op("vector", lambda e: e.tensor_scalar(
        out=a, in0=a, scalar1=INV_LN2, scalar2=-6.0, op0=ALU.mult, op1=ALU.max),
        wait=[("ac", ac2)])
    emit_floor(ctx, a, ian, fan, rr, ef)
    dvf = ctx.now("dv")
    ac3 = ctx.op("scalar", lambda e: e.activation(
        out=ui, in_=ef, func=AF.Exp, scale=-LN2, bias=cc(ctx, 4)[0:P_, :]), wait=[("dv", dvf)])
    ctx.op("vector", lambda e: e.tensor_tensor(out=rr, in0=w_ap, in1=ui, op=ALU.mult),
           wait=[("ac", ac3)])
    ctx.op("vector", lambda e: e.tensor_copy(out=ian, in_=rr))
    ctx.op("vector", lambda e: e.tensor_single_scalar(out=fan, in_=ian, scalar=-1.0, op=ALU.add))
    ctx.op("vector", lambda e: e.scalar_tensor_tensor(
        out=t1, in0=fan, scalar=1.0, in1=rr, op0=ALU.add, op1=ALU.subtract))
    ctx.op("vector", lambda e: e.scalar_tensor_tensor(
        out=fan, in0=t1, scalar=-0.5, in1=fan, op0=ALU.is_le, op1=ALU.add))
    dvk = ctx.op("vector", lambda e: e.scalar_tensor_tensor(
        out=fan, in0=t1, scalar=0.5, in1=fan, op0=ALU.is_le, op1=ALU.add))
    ac4 = ctx.op("scalar", lambda e: e.activation(
        out=ui, in_=ef, func=AF.Exp, scale=LN2, bias=cc(ctx, 5)[0:P_, :]), wait=[("dv", dvk)])
    return ctx.op("vector", lambda e: e.tensor_tensor(out=out_ap, in0=fan, in1=ui, op=ALU.mult),
                  wait=[("ac", ac4)])


def build_attn1(core):
    ctx = new_ctx()
    nc = ctx.nc
    x_in = nc.dram_tensor("x_in", [TPC, D], F32, kind="ExternalInput").ap()
    qf_in = nc.dram_tensor("qf_in", [2048, TPC], BF16, kind="ExternalInput").ap()
    kf_in = nc.dram_tensor("kf_in", [128, S], BF16, kind="ExternalInput").ap()
    vf_in = nc.dram_tensor("vf_in", [128, S], BF16, kind="ExternalInput").ap()
    wot_in = nc.dram_tensor("wot_in", [D, D], F32, kind="ExternalInput").ap()
    x3_out = nc.dram_tensor("x3_out", [TPC, D], F32, kind="ExternalOutput").ap()

    add_consts(ctx)
    onescol_f = ctx.sb("onescol_f", [128, 1], F32)
    onesrow_f = ctx.sb("onesrow_f", [1, 128], F32)
    ctx.op("vector", lambda e: e.memset(onescol_f[:], 1.0))
    ctx.op("vector", lambda e: e.memset(onesrow_f[:], 1.0))
    X = ctx.sb("X", [128, NT, D], F32)
    stat_m = ctx.sb("stat_m", [128, NT], F32)
    stat_r = ctx.sb("stat_r", [128, NT], F32)
    sq_scr = ctx.sb("sq_scr", [128, D], F32)
    WoIn = ctx.sb("WoIn", [128, 16, TPC], BF16)
    ld_x = emit_load_x(ctx, X, x_in)
    for t in range(NT):
        emit_ln_stats(ctx, X, t, stat_m, stat_r, sq_scr,
                      wait_v=[("ld_x", ld_x)], wait_s=[("ld_x", ld_x)])
    dv_stats = ctx.now("dv")

    ctx.push_scope()
    k_aug = [ctx.sb(f"kaug{g}", [65, S], BF16) for g in range(2)]
    vfh = [ctx.sb(f"vfh{g}", [64, S], BF16) for g in range(2)]
    vtok = [ctx.sb(f"vtok{g}", [128, 16, 65], BF16) for g in range(2)]
    qa_t = ctx.sb("qa_t", [65, TPC], BF16)
    PT = ctx.sb("PT", [128, 16, 512], BF16)
    ksq = ctx.sb("ksq", [64, S], BF16)
    norm_sc = ctx.sb("norm_sc", [1, 2048], F32)
    qn_sc = ctx.sb("qn_sc", [1, TPC], F32)
    c_sc = ctx.sb("c_sc", [1, 4], F32)
    rs_sc = ctx.sb("rs_sc", [1, 512], F32)
    rbx = ctx.sb("rbx", [64, 512], F32)
    lnx = ctx.sb("lnx", [128, D], F32)

    ps_sc = [Bank(ctx.ps("ps_sc0", [128, 512], F32)), Bank(ctx.ps("ps_sc1", [128, 512], F32))]
    ps_av = [Bank(ctx.ps("ps_av0", [65, 512], F32)), Bank(ctx.ps("ps_av1", [65, 512], F32))]
    ps_bc = Bank(ctx.ps("ps_bc", [64, 512], F32))
    pt = [Bank(ctx.ps("pt0", [128, 1024], BF16)), Bank(ctx.ps("pt1", [128, 1024], BF16))]

    ctx.op("vector", lambda e: e.memset(k_aug[0][64:65, :], 1.0))
    ctx.op("vector", lambda e: e.memset(k_aug[1][64:65, :], 1.0))
    ctx.op("vector", lambda e: e.memset(vtok[0][:, :, 64:65], 1.0))
    dv_ones = ctx.op("vector", lambda e: e.memset(vtok[1][:, :, 64:65], 1.0))

    for g in range(2):
        ctx.op("sync", lambda e, g=g: e.dma_start(
            out=k_aug[g][0:64, :], in_=kf_in[g * 64:(g + 1) * 64, :]),
            inc="ld_kv", inc_n=16, dma=True)
    ctx.op("sync", lambda e: e.dma_start(out=vfh[0][:], in_=vf_in[0:64, :]), inc="ld_kv", inc_n=16, dma=True)
    ctx.op("sync", lambda e: e.dma_start(out=vfh[1][:], in_=vf_in[64:128, :]), inc="ld_kv", inc_n=16, dma=True)
    ld_kv = ctx.now("ld_kv")
    for g in range(2):
        for kt in range(16):
            bank = pt[kt % 2]
            w = [("ld_kv", ld_kv), ("ld_c", ctx.consts_ld), ("dv", dv_ones)] + bank.guard()
            pe_t = ctx.op("tensor", lambda e, g=g, kt=kt, bank=bank: e.transpose(
                bank[:, 0:64], vfh[g][:, kt * 128:(kt + 1) * 128],
                ctx.ident[0:64, 0:64]), wait=w)
            bank.last_evac = ("dv", ctx.op("vector", lambda e, g=g, kt=kt, bank=bank: e.tensor_copy(
                out=vtok[g][:, kt, 0:64], in_=bank[:, 0:64]), wait=[("pe", pe_t)]))
    dv_vt = ctx.now("dv")

    ac_pre = 0
    for g in range(2):
        acq = ctx.op("scalar", lambda e, g=g: e.activation(
            out=ksq[:], in_=k_aug[g][0:64, :], func=AF.Square), wait=[("ld_kv", ld_kv)])
        for blk in range(4):
            pe = ctx.op("tensor", lambda e, blk=blk: e.matmul(
                ps_sc[0][0:1, :], ctx.ones_col[0:64, :], ksq[:, blk * 512:(blk + 1) * 512],
                start=True, stop=True),
                wait=[("ac", acq)] + ps_sc[0].guard())
            ps_sc[0].last_evac = ("dv", ctx.op("vector", lambda e, blk=blk: e.tensor_copy(
                out=norm_sc[:, blk * 512:(blk + 1) * 512], in_=ps_sc[0][0:1, :]),
                wait=[("pe", pe)]))
        ctx.op("vector", lambda e, g=g: e.reduce_max(
            out=c_sc[:, g:g + 1], in_=norm_sc[:], axis=AX))
        acc = ctx.op("scalar", lambda e, g=g: e.activation(
            out=c_sc[:, g:g + 1], in_=c_sc[:, g:g + 1], func=AF.Sqrt),
            wait=[("dv", ctx.now("dv"))])
        ctx.op("vector", lambda e, g=g: e.tensor_scalar_mul(
            out=c_sc[:, 2 + g:3 + g], in0=c_sc[:, g:g + 1], scalar1=-1.0),
            wait=[("ac", acc)])
    dv_negc = ctx.now("dv")

    last_q_pe = 0
    prev = {}
    for h in range(32):
        g = h // 16
        lh = f"lq{h % 2}"
        ctx.op("sync", lambda e, h=h: e.dma_start(
            out=qa_t[0:64, :], in_=qf_in[h * 64:(h + 1) * 64, :]),
            wait=[("pe", last_q_pe)], inc=lh, inc_n=16, dma=True)
        ldq = ctx.now(lh)
        acq2 = ctx.op("scalar", lambda e: e.activation(
            out=ksq[:, 0:TPC], in_=qa_t[0:64, :], func=AF.Square),
            wait=[(lh, ldq), ("pe", prev.get("ksq_pe", 0))])
        for blk in range(2):
            pe = ctx.op("tensor", lambda e, blk=blk: e.matmul(
                ps_sc[0][0:1, :], ctx.ones_col[0:64, :], ksq[:, blk * 512:(blk + 1) * 512],
                start=True, stop=True),
                wait=[("ac", acq2)] + ps_sc[0].guard())
            ps_sc[0].last_evac = ("dv", ctx.op("vector", lambda e, blk=blk: e.tensor_copy(
                out=qn_sc[:, blk * 512:(blk + 1) * 512], in_=ps_sc[0][0:1, :]),
                wait=[("pe", pe)]))
        prev["ksq_pe"] = pe
        acq3 = ctx.op("scalar", lambda e: e.activation(
            out=qn_sc[:], in_=qn_sc[:], func=AF.Sqrt),
            wait=[ps_sc[0].last_evac])
        dvb = ctx.op("vector", lambda e, g=g: e.tensor_single_scalar(
            out=qa_t[64:65, :], in_=qn_sc[:], scalar=c_sc[:, 2 + g:3 + g], op=ALU.mult),
            wait=[("ac", acq3), ("dv", dv_negc)])
        for qb in range(2):
            av_bank = ps_av[qb]
            for kt in range(16):
                bank = ps_sc[kt % 2]
                w = bank.guard() + [("dv", dvb)]
                if kt == 0:
                    w.append(("pe", prev.get(f"avdone{qb}", 0)))
                pe = ctx.op("tensor", lambda e, g=g, kt=kt, qb=qb, bank=bank: e.matmul(
                    bank[:], k_aug[g][:, kt * 128:(kt + 1) * 128],
                    qa_t[:, qb * 512:(qb + 1) * 512], start=True, stop=True), wait=w)
                bank.last_evac = ("ac", ctx.op("scalar", lambda e, bank=bank, kt=kt: e.activation(
                    out=PT[:, kt, :], in_=bank[:], func=AF.Exp), wait=[("pe", pe)]))
            ac_exp = ctx.now("ac")
            for kt in range(16):
                w = []
                if kt == 0:
                    w = [("ac", ac_exp), ("dv", prev.get(f"avused{qb}", 0)), ("dv", dv_vt)]
                pe_av = ctx.op("tensor", lambda e, g=g, kt=kt, av_bank=av_bank: e.matmul(
                    av_bank[:], vtok[g][:, kt, :], PT[:, kt, :],
                    start=(kt == 0), stop=(kt == 15)), wait=w)
            prev[f"avdone{qb}"] = pe_av
            dvr = ctx.op("vector", lambda e, av_bank=av_bank: e.reciprocal(
                out=rs_sc[:], in_=av_bank[64:65, :]), wait=[("pe", pe_av)])
            pe_b = ctx.op("tensor", lambda e: e.matmul(
                ps_bc[:], onesrow_f[:, 0:64], rs_sc[:], start=True, stop=True),
                wait=[("dv", dvr)] + ps_bc.guard())
            ps_bc.last_evac = ("ac", ctx.op("scalar", lambda e: e.copy(
                out=rbx[:], in_=ps_bc[:]), wait=[("pe", pe_b)]))
            prev[f"avused{qb}"] = ctx.op("vector",
                lambda e, h=h, qb=qb, av_bank=av_bank: e.tensor_tensor(
                    out=WoIn[(h % 2) * 64:(h % 2) * 64 + 64, h // 2,
                             qb * 512:(qb + 1) * 512],
                    in0=av_bank[0:64, :], in1=rbx[:], op=ALU.mult),
                wait=[ps_bc.last_evac])
        last_q_pe = ctx.now("pe")
    dv_woin = ctx.now("dv")

    ctx.pop_scope()
    reuse_bar = ctx.barrier_wait()
    wob = [ctx.sb(f"wob{i}", [128, DC, 512], BF16) for i in range(2)]
    osb = ctx.sb("osb", [128, 16, TPC], BF16)
    lnx2 = ctx.sb("lnx2", [128, D], F32)
    pw = [Bank(ctx.ps("pw0", [128, 512], F32)), Bank(ctx.ps("pw1", [128, 512], F32))]
    ptw = [Bank(ctx.ps("ptw0", [128, 1024], BF16)), Bank(ctx.ps("ptw1", [128, 1024], BF16))]
    wsrc = wot_in.rearrange("(c p) f -> p c f", p=128)
    last_wo_pe = {}
    for ob in range(4):
        wb = wob[ob % 2]
        wld = ctx.op("gpsimd", lambda e, ob=ob, wb=wb: e.dma_start(
            out=wb[:], in_=wsrc[:, :, ob * 512:(ob + 1) * 512]),
            wait=list(reuse_bar) + [("pe", last_wo_pe.get(ob % 2, 0))],
            inc=f"wl{ob % 2}", inc_n=16, dma=True)
        wldc = ctx.now(f"wl{ob % 2}")
        for fc in range(4):
            for qbh in range(2):
                bank = pw[(fc * 2 + qbh) % 2]
                w = list(bank.guard())
                if fc == 0 and qbh == 0:
                    w += [(f"wl{ob % 2}", wldc), ("dv", dv_woin)]
                for d in range(DC):
                    pe = ctx.op("tensor", lambda e, wb=wb, d=d, fc=fc, qbh=qbh, bank=bank:
                                e.matmul(bank[:], wb[:, d, fc * 128:(fc + 1) * 128],
                                         WoIn[:, d, qbh * 512:(qbh + 1) * 512],
                                         start=(d == 0), stop=(d == DC - 1)),
                                wait=w if d == 0 else ())
                bank.last_evac = ("ac", ctx.op("scalar",
                    lambda e, bank=bank, ob=ob, fc=fc, qbh=qbh: e.copy(
                        out=osb[:, ob * 4 + fc, qbh * 512:(qbh + 1) * 512], in_=bank[:]),
                    wait=[("pe", pe)]))
        last_wo_pe[ob % 2] = ctx.now("pe")
    ac_osb = ctx.now("ac")

    for t in range(NT):
        dvl = ctx.op("vector", lambda e, t=t: e.tensor_scalar(
            out=lnx2[:], in0=X[:, t, :], scalar1=stat_m[:, t:t + 1],
            scalar2=stat_r[:, t:t + 1], op0=ALU.subtract, op1=ALU.mult),
            wait=[("dv", dv_stats)])
        ctx.op("vector", lambda e, t=t: e.tensor_tensor(
            out=X[:, t, :], in0=X[:, t, :], in1=lnx2[:], op=ALU.add))
        for gg in range(2):
            bank = ptw[gg % 2]
            w = [("ac", ac_osb)] + bank.guard()
            for j in range(8):
                d = gg * 8 + j
                pe = ctx.op("tensor", lambda e, d=d, j=j, t=t, bank=bank: e.transpose(
                    bank[:, j * 128:(j + 1) * 128], osb[:, d, t * 128:(t + 1) * 128],
                    ctx.ident[:]), wait=w if j == 0 else ())
            bank.last_evac = ("dv", ctx.op("vector", lambda e, t=t, gg=gg, bank=bank: e.tensor_tensor(
                out=X[:, t, gg * 1024:(gg + 1) * 1024],
                in0=X[:, t, gg * 1024:(gg + 1) * 1024],
                in1=bank[:], op=ALU.add), wait=[("pe", pe)]))
        ctx.op("sync", lambda e, t=t: e.dma_start(
            out=x3_out.rearrange("(t p) d -> p t d", p=128)[:, t, :], in_=X[:, t, :]),
            wait=[("dv", ctx.now("dv"))], inc="st_out", inc_n=16, dma=True)
    ctx.wait("sync", [("st_out", ctx.now("st_out"))])
    ctx.build_and_close()
    return ctx.nc


def build_wprep1a(core):
    ctx = new_ctx()
    nc = ctx.nc
    wg = nc.dram_tensor("wg_nat", [704, 2048], F32, kind="ExternalInput").ap()
    wu = nc.dram_tensor("wu_nat", [704, 2048], F32, kind="ExternalInput").ap()
    wd = nc.dram_tensor("wd_nat", [256, 5632], F32, kind="ExternalInput").ap()
    og = nc.dram_tensor("wscg_out", [704, 1], F32, kind="ExternalOutput").ap()
    ou = nc.dram_tensor("wscu_out", [704, 1], F32, kind="ExternalOutput").ap()
    od = nc.dram_tensor("wscd_out", [256, 1], F32, kind="ExternalOutput").ap()
    tile = ctx.sb("tile", [128, 5632], F32)
    red = ctx.sb("red", [128, 1], F32)
    for (src, dst, rows, cols) in ((wg, og, 704, 2048), (wu, ou, 704, 2048),
                                   (wd, od, 256, 5632)):
        r0 = 0
        while r0 < rows:
            p = min(128, rows - r0)
            lsem = f"l{r0 % 256}"
            lc = ctx.op("sync", lambda e, src=src, r0=r0, p=p, cols=cols: e.dma_start(
                out=tile[0:p, 0:cols], in_=src[r0:r0 + p, :]),
                wait=[("st_o", ctx.now("st_o"))], inc=lsem, inc_n=16, dma=True)
            dv = ctx.op("vector", lambda e, p=p, cols=cols: e.reduce_max(
                out=red[0:p, :], in_=tile[0:p, 0:cols], axis=AX,
                apply_absolute_value=True), wait=[(lsem, lc)])
            dv = ctx.op("vector", lambda e, p=p: e.tensor_scalar_mul(
                out=red[0:p, :], in0=red[0:p, :], scalar1=1.0 / 448.0))
            ctx.op("sync", lambda e, dst=dst, r0=r0, p=p: e.dma_start(
                out=dst[r0:r0 + p, :], in_=red[0:p, :]),
                wait=[("dv", dv)], inc="st_o", inc_n=16, dma=True)
            r0 += p
    ctx.wait("sync", [("st_o", ctx.now("st_o"))])
    ctx.build_and_close()
    return ctx.nc


def emit_bcast_row(ctx, row_ap, dst_sb, F, ps_bank, ones_row_f, wait=()):
    last = 0
    for j in range(0, F, 512):
        wj = min(512, F - j)
        pe = ctx.op("tensor", lambda e, j=j, wj=wj: e.matmul(
            ps_bank[:, 0:wj], ones_row_f[:], row_ap[:, j:j + wj],
            start=True, stop=True), wait=list(wait) + ps_bank.guard())
        ps_bank.last_evac = ("ac", ctx.op("scalar", lambda e, j=j, wj=wj: e.copy(
            out=dst_sb[:, j:j + wj], in_=ps_bank[:, 0:wj]), wait=[("pe", pe)]))
        last = ps_bank.last_evac[1]
    return last


def build_wprep1b(core):
    ctx = new_ctx()
    nc = ctx.nc
    wgT = nc.dram_tensor("wgT_sh", [256, 5632], F32, kind="ExternalInput").ap()
    wuT = nc.dram_tensor("wuT_sh", [256, 5632], F32, kind="ExternalInput").ap()
    wdT = nc.dram_tensor("wdT_sh", [704, 2048], F32, kind="ExternalInput").ap()
    scg = nc.dram_tensor("wscg_full", [1, 5632], F32, kind="ExternalInput").ap()
    scu = nc.dram_tensor("wscu_full", [1, 5632], F32, kind="ExternalInput").ap()
    scd = nc.dram_tensor("wscd_full", [1, 2048], F32, kind="ExternalInput").ap()
    qg = nc.dram_tensor("qwgT_sh", [256, 5632], BF16, kind="ExternalOutput").ap()
    qu = nc.dram_tensor("qwuT_sh", [256, 5632], BF16, kind="ExternalOutput").ap()
    qd = nc.dram_tensor("qwdT_sh", [704, 2048], BF16, kind="ExternalOutput").ap()

    add_consts(ctx)
    onesrow_f = ctx.sb("onesrow_f", [1, 128], F32)
    ctx.op("vector", lambda e: e.memset(onesrow_f[:], 1.0))
    scrow = ctx.sb("scrow", [1, 5632], F32)
    rrow = ctx.sb("rrow", [1, 5632], F32)
    bc_s = ctx.sb("bc_s", [128, 5632], F32)
    bc_w = ctx.sb("bc_w", [128, 5632], F32)
    tile = ctx.sb("tile", [128, 5632], F32)
    outb = ctx.sb("outb", [128, 5632], BF16)
    QB = 1024
    qs_t = {k: ctx.sb(f"qs_{k}", [128, QB], F32 if k != "ian" else I32)
            for k in ("a", "ian", "fan", "rr", "ef", "ui", "t1")}
    ps_b = Bank(ctx.ps("ps_b", [128, 512], F32))

    for (wsrc, scsrc, qdst, rows, F) in ((wgT, scg, qg, 256, 5632),
                                         (wuT, scu, qu, 256, 5632),
                                         (wdT, scd, qd, 704, 2048)):
        lsc = ctx.op("sync", lambda e, scsrc=scsrc, F=F: e.dma_start(
            out=scrow[:, 0:F], in_=scsrc), inc="lsc", inc_n=16, dma=True)
        dvv = ctx.op("vector", lambda e, F=F: e.reciprocal(
            out=rrow[:, 0:F], in_=scrow[:, 0:F]), wait=[("lsc", lsc), ("ac", ctx.now("ac"))])
        emit_bcast_row(ctx, rrow[:, 0:F], bc_s, F, ps_b, onesrow_f,
                       wait=[("dv", dvv), ("ld_c", ctx.consts_ld)])
        ac_b = emit_bcast_row(ctx, scrow[:, 0:F], bc_w, F, ps_b, onesrow_f,
                              wait=[("dv", dvv)])
        r0 = 0
        while r0 < rows:
            p = min(128, rows - r0)
            lw = f"lw{(r0 // 128) % 2}"
            lc = ctx.op("sync", lambda e, wsrc=wsrc, r0=r0, p=p, F=F: e.dma_start(
                out=tile[0:p, 0:F], in_=wsrc[r0:r0 + p, :]),
                wait=[("st_o", ctx.now("st_o"))], inc=lw, inc_n=16, dma=True)
            for j in range(0, F, QB):
                wj = min(QB, F - j)
                dvw = ctx.op("vector", lambda e, j=j, wj=wj, p=p: e.tensor_tensor(
                    out=tile[0:p, j:j + wj],
                    in0=tile[0:p, j:j + wj], in1=bc_s[0:p, j:j + wj], op=ALU.mult),
                    wait=[(lw, lc), ("ac", ac_b)])
                qv = {k: qs_t[k][0:p, 0:wj] for k in qs_t}
                emit_fp8_quant(ctx, tile[0:p, j:j + wj], outb[0:p, j:j + wj], qv,
                               wait=[("dv", dvw)])
                dvq = ctx.op("vector", lambda e, j=j, wj=wj, p=p: e.tensor_tensor(
                    out=outb[0:p, j:j + wj], in0=outb[0:p, j:j + wj],
                    in1=bc_w[0:p, j:j + wj], op=ALU.mult))
            ctx.op("sync", lambda e, qdst=qdst, r0=r0, p=p, F=F: e.dma_start(
                out=qdst[r0:r0 + p, :], in_=outb[0:p, 0:F]),
                wait=[("dv", ctx.now("dv"))], inc="st_o", inc_n=16, dma=True)
            r0 += p
    ctx.wait("sync", [("st_o", ctx.now("st_o"))])
    ctx.build_and_close()
    return ctx.nc


def build_ffn1(core):
    ctx = new_ctx()
    nc = ctx.nc
    I = 5632
    NC_ = I // 128
    x_in = nc.dram_tensor("x_in", [TPC, D], F32, kind="ExternalInput").ap()
    qwg_in = nc.dram_tensor("qwg_in", [D, I], BF16, kind="ExternalInput").ap()
    qwu_in = nc.dram_tensor("qwu_in", [D, I], BF16, kind="ExternalInput").ap()
    qwd_in = nc.dram_tensor("qwd_in", [I, D], BF16, kind="ExternalInput").ap()
    sig_in = nc.dram_tensor("sig_in", [I], I32, kind="ExternalInput").ap()
    y_out = nc.dram_tensor("y_out", [TPC, D], F32, kind="ExternalOutput").ap()
    w1T = nc.dram_tensor("w1T", [I, D], BF16, kind="Internal").ap()

    add_consts(ctx)
    X = ctx.sb("X", [128, NT, D], F32)
    HT = ctx.sb("HT", [128, DC, TPC], BF16)
    stat_m = ctx.sb("stat_m", [128, NT], F32)
    stat_r = ctx.sb("stat_r", [128, NT], F32)
    sig1 = ctx.sb("sig1", [128, NT], F32)
    sig2 = ctx.sb("sig2", [128, NT], F32)
    s448 = ctx.sb("s448", [128, NT], F32)
    sq_scr = ctx.sb("sq_scr", [128, D], F32)
    ld_x = emit_load_x(ctx, X, x_in)

    scat_t = ctx.sb("scat_t", [128, D], BF16)
    sig_sb = ctx.sb("sig_sb", [128, 1], I32)
    for c in range(NC_):
        lsem = f"sc{c % 2}"
        lc = ctx.op("sync", lambda e, c=c: e.dma_start(
            out=scat_t[:], in_=qwd_in[c * 128:(c + 1) * 128, :]),
            wait=[("scat", ctx.now("scat"))], inc=lsem, inc_n=16, dma=True)
        lc2 = ctx.op("sync", lambda e, c=c: e.dma_start(
            out=sig_sb[:], in_=sig_in[c * 128:(c + 1) * 128, None]),
            inc=lsem, inc_n=16, dma=True)
        ctx.op("gpsimd", lambda e, c=c: e.indirect_dma_start(
            out=w1T, out_offset=IndirectOffsetOnAxis(sig_sb[:], 0),
            in_=scat_t[:], in_offset=None),
            wait=[(lsem, lc2)], inc="scat", inc_n=16, dma=True)
    scat_done = ctx.now("scat")

    ctx.push_scope()
    QB = 1024
    qs_t = {k: ctx.sb(f"qs_{k}", [128, QB], F32 if k != "ian" else I32)
            for k in ("a", "ian", "fan", "rr", "ef", "ui", "t1")}
    xm = ctx.sb("xm", [128, D], F32)
    qxb = ctx.sb("qxb", [128, D], BF16)
    pt = [Bank(ctx.ps("pt0", [128, 1024], BF16)), Bank(ctx.ps("pt1", [128, 1024], BF16))]
    ctx.wait("vector", [("ld_c", ctx.consts_ld)])
    for t in range(NT):
        emit_ln_stats(ctx, X, t, stat_m, stat_r, sq_scr,
                      wait_v=[("ld_x", ld_x)], wait_s=[("ld_x", ld_x)])
        dvxm = ctx.op("vector", lambda e, t=t: e.tensor_single_scalar(
            out=xm[:], in_=X[:, t, :], scalar=stat_m[:, t:t + 1], op=ALU.subtract))
        dvm = ctx.op("vector", lambda e, t=t: e.reduce_max(
            out=s448[:, t:t + 1], in_=xm[:], axis=AX, apply_absolute_value=True))
        ctx.op("vector", lambda e, t=t: e.scalar_tensor_tensor(
            out=sig1[:, t:t + 1], in0=s448[:, t:t + 1], scalar=1.0 / 448.0,
            in1=stat_r[:, t:t + 1], op0=ALU.mult, op1=ALU.mult))
        ctx.op("vector", lambda e, t=t: e.reciprocal(
            out=s448[:, t:t + 1], in_=s448[:, t:t + 1]))
        dvs = ctx.op("vector", lambda e, t=t: e.tensor_scalar_mul(
            out=s448[:, t:t + 1], in0=s448[:, t:t + 1], scalar1=448.0))
        for j in range(0, D, QB):
            dvw = ctx.op("vector", lambda e, t=t, j=j: e.tensor_single_scalar(
                out=xm[:, j:j + QB], in_=xm[:, j:j + QB], scalar=s448[:, t:t + 1],
                op=ALU.mult), wait=[("dv", dvs)])
            qv = {k: qs_t[k][:] for k in qs_t}
            emit_fp8_quant(ctx, xm[:, j:j + QB], qxb[:, j:j + QB], qv,
                           wait=[("dv", dvw)])
        emit_transpose_tile(ctx, qxb[:], HT, t, pt, wait_pe=[("dv", ctx.now("dv"))])
    dv_qxt = ctx.now("dv")
    ctx.pop_scope()
    barA = ctx.barrier_wait()

    ctx.push_scope()
    OB = 256
    NOB = I // OB
    qs_t = {k: ctx.sb(f"q2_{k}", [128, QB], F32 if k != "ian" else I32)
            for k in ("a", "ian", "fan", "rr", "ef", "ui", "t1")}
    wgb_t = ctx.sb("wgb", [128, DC, OB], BF16)
    wub_t = ctx.sb("wub", [128, DC, OB], BF16)
    wgb = [wgb_t, wgb_t]
    wub = [wub_t, wub_t]
    Tq = ctx.sb("Tq", [128, 2, I], BF16)
    qtb = ctx.sb("qtb", [128, QB], BF16)
    qtT = ctx.sb("qtT", [128, NC_, 256], BF16)
    w1c = [ctx.sb(f"w1c{i}", [128, 1024], BF16) for i in range(2)]
    gsb = ctx.sb("gsb", [128, OB], F32)
    usb = ctx.sb("usb", [128, OB], F32)
    rmx = ctx.sb("rmx", [128, 1], F32)
    psg = Bank(ctx.ps("psg", [128, 512], F32))
    psu = Bank(ctx.ps("psu", [128, 512], F32))
    ptq = [Bank(ctx.ps("ptq0", [128, 1024], BF16)), Bank(ctx.ps("ptq1", [128, 1024], BF16))]
    psd = [Bank(ctx.ps(f"psd{i}", [128, 512], F32)) for i in range(4)]
    wg_src = qwg_in.rearrange("(c p) f -> p c f", p=128)
    wu_src = qwu_in.rearrange("(c p) f -> p c f", p=128)
    last_gu_pe = {}
    w1_use = {}
    for quarter in range(4):
        tts = [quarter * 2, quarter * 2 + 1]
        for ob in range(NOB):
            lw = f"gw{ob % 2}"
            wld = ctx.op("gpsimd", lambda e, ob=ob: e.dma_start(
                out=wgb[ob % 2][:], in_=wg_src[:, :, ob * OB:(ob + 1) * OB]),
                wait=(list(barA) if (quarter == 0 and ob < 2) else []) +
                     [("pe", last_gu_pe.get("w", 0))],
                inc=lw, inc_n=16, dma=True)
            ctx.op("gpsimd", lambda e, ob=ob: e.dma_start(
                out=wub[ob % 2][:], in_=wu_src[:, :, ob * OB:(ob + 1) * OB]),
                inc=lw, inc_n=16, dma=True)
            lwc = ctx.now(lw)
            for ti, tt in enumerate(tts):
                wmm = [(lw, lwc), ("dv", dv_qxt)] + psg.guard() + psu.guard()
                for d in range(DC):
                    ctx.op("tensor", lambda e, d=d, tt=tt, ob=ob: e.matmul(
                        psg[:, 0:OB], HT[:, d, tt * 128:(tt + 1) * 128], wgb[ob % 2][:, d, :],
                        start=(d == 0), stop=(d == DC - 1)),
                        wait=wmm if d == 0 else ())
                    pe_mm = ctx.op("tensor", lambda e, d=d, tt=tt, ob=ob: e.matmul(
                        psu[:, 0:OB], HT[:, d, tt * 128:(tt + 1) * 128], wub[ob % 2][:, d, :],
                        start=(d == 0), stop=(d == DC - 1)))
                psg.last_evac = ("ac", ctx.op("scalar", lambda e, tt=tt: e.activation(
                    out=gsb[:], in_=psg[:, 0:OB], func=AF.Silu, scale=sig1[:, tt:tt + 1]),
                    wait=[("pe", pe_mm), ("dv", last_gu_pe.get("tq", 0))]))
                psu.last_evac = ("dv", ctx.op("vector", lambda e, tt=tt: e.tensor_single_scalar(
                    out=usb[:], in_=psu[:, 0:OB], scalar=sig1[:, tt:tt + 1], op=ALU.mult),
                    wait=[("pe", pe_mm)]))
                last_gu_pe["tq"] = ctx.op("vector", lambda e, ti=ti, ob=ob: e.tensor_tensor(
                    out=Tq[:, ti, ob * OB:(ob + 1) * OB], in0=gsb[:], in1=usb[:],
                    op=ALU.mult), wait=[psg.last_evac])
            last_gu_pe["w"] = ctx.now("pe")
        for ti, tt in enumerate(tts):
            for cb in range(0, I, QB):
                wj = min(QB, I - cb)
                dvr = ctx.op("vector", lambda e, ti=ti, cb=cb, wj=wj: e.reduce_max(
                    out=qs_t["a"][:, 0:1], in_=Tq[:, ti, cb:cb + wj], axis=AX,
                    apply_absolute_value=True))
                if cb == 0:
                    ctx.op("vector", lambda e: e.tensor_copy(out=rmx[:], in_=qs_t["a"][:, 0:1]))
                else:
                    ctx.op("vector", lambda e: e.tensor_tensor(
                        out=rmx[:], in0=rmx[:], in1=qs_t["a"][:, 0:1], op=ALU.max))
            ctx.op("vector", lambda e, tt=tt: e.tensor_scalar_mul(
                out=sig2[:, tt:tt + 1], in0=rmx[:], scalar1=1.0 / 448.0))
            ctx.op("vector", lambda e: e.reciprocal(out=rmx[:], in_=rmx[:]))
            dvs2 = ctx.op("vector", lambda e: e.tensor_scalar_mul(
                out=rmx[:], in0=rmx[:], scalar1=448.0))
            for cb in range(0, I, QB):
                wj = min(QB, I - cb)
                dvw = ctx.op("vector", lambda e, ti=ti, cb=cb, wj=wj: e.tensor_single_scalar(
                    out=qs_t["t1"][:, 0:wj], in_=Tq[:, ti, cb:cb + wj], scalar=rmx[:],
                    op=ALU.mult), wait=[("dv", dvs2)])
                qv = {k: qs_t[k][:, 0:wj] for k in qs_t}
                dvq = emit_fp8_quant(ctx, qs_t["t1"][:, 0:wj], qtb[:, 0:wj], qv,
                                     wait=[("dv", dvw)])
                nch = wj // 128
                for hb in range(0, nch, 8):
                    bank = ptq[(hb // 8) % 2]
                    nd = min(8, nch - hb)
                    wtr = [("dv", dvq)] + bank.guard()
                    for j in range(nd):
                        pe_t = ctx.op("tensor", lambda e, j=j, hb=hb, bank=bank: e.transpose(
                            bank[:, j * 128:(j + 1) * 128],
                            qtb[:, (hb + j) * 128:(hb + j + 1) * 128], ctx.ident[:]),
                            wait=wtr if j == 0 else ())
                    bank.last_evac = ("dv", ctx.op("vector",
                        lambda e, bank=bank, cb=cb, hb=hb, nd=nd, ti=ti: e.tensor_copy(
                            out=qtT[:, cb // 128 + hb:cb // 128 + hb + nd,
                                    ti * 128:(ti + 1) * 128],
                            in_=bank[:, 0:nd * 128].rearrange("p (j c) -> p j c", c=128)),
                        wait=[("pe", pe_t)]))
        dv_qtT = ctx.now("dv")
        for ohalf in range(2):
            for c in range(NC_):
                lw1 = f"dw{c % 2}"
                wfirst = [("scat", scat_done)] if (quarter == 0 and ohalf == 0 and c < 2) else []
                ctx.op("gpsimd", lambda e, c=c, ohalf=ohalf: e.dma_start(
                    out=w1c[c % 2][:], in_=w1T[c * 128:(c + 1) * 128,
                                               ohalf * 1024:(ohalf + 1) * 1024]),
                    wait=wfirst + [("pe", w1_use.get(c % 2, 0))], inc=lw1, inc_n=16, dma=True)
                lw1c = ctx.now(lw1)
                for ti in range(2):
                    for j in range(2):
                        wmm = []
                        if ti == 0 and j == 0:
                            wmm = [(lw1, lw1c), ("dv", dv_qtT)]
                            if c == 0:
                                for b in psd:
                                    wmm += b.guard()
                        pe_d = ctx.op("tensor", lambda e, c=c, ti=ti, j=j, ohalf=ohalf:
                                      e.matmul(psd[ti * 2 + j][:],
                                               qtT[:, c, ti * 128:(ti + 1) * 128],
                                               w1c[c % 2][:, j * 512:(j + 1) * 512],
                                               start=(c == 0), stop=(c == NC_ - 1)),
                                      wait=wmm)
                w1_use[c % 2] = ctx.now("pe")
            for ti, tt in enumerate(tts):
                for j in range(2):
                    psd[ti * 2 + j].last_evac = ("dv", ctx.op("vector",
                        lambda e, tt=tt, ti=ti, j=j, ohalf=ohalf: e.scalar_tensor_tensor(
                            out=X[:, tt, ohalf * 1024 + j * 512:ohalf * 1024 + (j + 1) * 512],
                            in0=psd[ti * 2 + j][:], scalar=sig2[:, tt:tt + 1],
                            in1=X[:, tt, ohalf * 1024 + j * 512:ohalf * 1024 + (j + 1) * 512],
                            op0=ALU.mult, op1=ALU.add),
                        wait=[("pe", ctx.now("pe"))] if (ti == 0 and j == 0) else ()))
        for tt in tts:
            ctx.op("sync", lambda e, tt=tt: e.dma_start(
                out=y_out.rearrange("(t p) d -> p t d", p=128)[:, tt, :], in_=X[:, tt, :]),
                wait=[("dv", ctx.now("dv"))], inc="st_out", inc_n=16, dma=True)
    ctx.wait("sync", [("st_out", ctx.now("st_out"))])
    ctx.build_and_close()
    return ctx.nc



def _carve(x_full, core):
    seq = core // 2
    parts = [np.asarray(x_full[seq, lo:hi]) for lo, hi in chunk_rows(core)]
    return np.ascontiguousarray(np.concatenate(parts, axis=0))


def _uncarve(y_slices):
    out = np.empty((B, S, D), dtype=y_slices[0].dtype)
    for core in range(8):
        seq = core // 2
        r0 = 0
        for lo, hi in chunk_rows(core):
            n = hi - lo
            out[seq, lo:hi] = y_slices[core][r0:r0 + n]
            r0 += n
    return out


def _seq_order_T(arr_T_slices):
    seqs = []
    for b in range(B):
        even, odd = arr_T_slices[2 * b], arr_T_slices[2 * b + 1]
        full = np.empty((even.shape[0], S), dtype=even.dtype)
        full[:, 0:512] = even[:, 0:512]
        full[:, 1536:2048] = even[:, 512:1024]
        full[:, 512:1536] = odd
        seqs.append(full)
    return seqs


_NC_CACHE = {}


def _get_nc(name, builder, *args):
    key = (name,) + args
    if key not in _NC_CACHE:
        _NC_CACHE[key] = builder(*args)
    return _NC_CACHE[key]


def kernel_v1_multilaunch(L0_attn_x, L0_attn_W_qkv, L0_attn_W_o, L0_attn_lambda_q1,
           L0_attn_lambda_k1, L0_attn_lambda_q2, L0_attn_lambda_k2,
           L0_attn_gn_gamma, L0_attn_gn_beta, L0_ffn_W_g, L0_ffn_W_u,
           L0_ffn_W_d, L1_attn_norm_scale, L1_attn_split_scale, L1_attn_Wq,
           L1_attn_Wk, L1_attn_Wv, L1_attn_Wo, L1_ffn_w_gate, L1_ffn_w_up,
           L1_ffn_w_down, L1_ffn_up_reorder_index, L1_ffn_down_reorder_index):
    f32 = np.float32
    x = np.ascontiguousarray(np.asarray(L0_attn_x, f32))
    xs = [_carve(x, c) for c in range(8)]

    wqkvT = np.ascontiguousarray(np.asarray(L0_attn_W_qkv, f32).T)
    nc = _get_nc("qkv", build_qkv_launch, 0, 0)
    res = run_bass_kernel_spmd(nc, [{"x_in": xs[c], "wt_in": wqkvT}
                                    for c in range(8)], core_ids=list(range(8)))
    qkv_T = [np.asarray(r["qkv_out"]) for r in res.results]
    kf_seq = _seq_order_T([q[2048:4096] for q in qkv_T])
    vf_seq = _seq_order_T([q[4096:6144] for q in qkv_T])

    lam4 = np.ascontiguousarray(np.stack([
        np.asarray(L0_attn_lambda_q1, f32), np.asarray(L0_attn_lambda_k1, f32),
        np.asarray(L0_attn_lambda_q2, f32), np.asarray(L0_attn_lambda_k2, f32)]))
    gnt = np.ascontiguousarray(np.concatenate(
        [np.asarray(L0_attn_gn_gamma, f32).T, np.asarray(L0_attn_gn_beta, f32).T], axis=1))
    wot0 = np.ascontiguousarray(np.asarray(L0_attn_W_o, f32).T)
    x1 = [None] * 8
    for par in range(2):
        cores = [c for c in range(8) if c % 2 == par]
        nc = _get_nc("attn0", build_attn0, par)
        res = run_bass_kernel_spmd(nc, [
            {"x_in": xs[c], "qf_in": np.ascontiguousarray(qkv_T[c][0:2048]),
             "kf_in": kf_seq[c // 2], "vf_in": vf_seq[c // 2],
             "lam_in": lam4, "gnt_in": gnt, "wot_in": wot0}
            for c in cores], core_ids=cores)
        for i, c in enumerate(cores):
            x1[c] = np.asarray(res.results[i]["x1_out"])

    wgt = np.ascontiguousarray(np.asarray(L0_ffn_W_g, f32).T)
    wut = np.ascontiguousarray(np.asarray(L0_ffn_W_u, f32).T)
    wdt = np.ascontiguousarray(np.asarray(L0_ffn_W_d, f32).T)
    nc = _get_nc("ffn0", build_ffn0, 0)
    res = run_bass_kernel_spmd(nc, [{"x_in": x1[c], "wgt_in": wgt,
                                     "wut_in": wut, "wdt_in": wdt}
                                    for c in range(8)], core_ids=list(range(8)))
    x2 = [np.asarray(r["x2_out"]) for r in res.results]

    wcatT = np.ascontiguousarray(np.concatenate(
        [np.asarray(L1_attn_Wq, f32), np.asarray(L1_attn_Wk, f32),
         np.asarray(L1_attn_Wv, f32)], axis=0).T)
    ns = np.ascontiguousarray(np.asarray(L1_attn_norm_scale, f32)[None, :])
    ss = np.ascontiguousarray(np.asarray(L1_attn_split_scale, f32)[None, :])
    nc = _get_nc("qkv1", build_qkv_launch, 0, 1)
    res = run_bass_kernel_spmd(nc, [{"x_in": x2[c], "wt_in": wcatT,
                                     "ns_in": ns, "ss_in": ss}
                                    for c in range(8)], core_ids=list(range(8)))
    qkv1_T = [np.asarray(r["qkv_out"]) for r in res.results]
    kf1_seq = _seq_order_T([q[2048:2176] for q in qkv1_T])
    vf1_seq = _seq_order_T([q[2176:2304] for q in qkv1_T])
    wot1 = np.ascontiguousarray(np.asarray(L1_attn_Wo, f32).T)
    nc = _get_nc("attn1", build_attn1, 0)
    res = run_bass_kernel_spmd(nc, [
        {"x_in": x2[c], "qf_in": np.ascontiguousarray(qkv1_T[c][0:2048]),
         "kf_in": kf1_seq[c // 2], "vf_in": vf1_seq[c // 2], "wot_in": wot1}
        for c in range(8)], core_ids=list(range(8)))
    x3 = [np.asarray(r["x3_out"]) for r in res.results]

    wg_n = np.asarray(L1_ffn_w_gate, f32)
    wu_n = np.asarray(L1_ffn_w_up, f32)
    wd_n = np.asarray(L1_ffn_w_down, f32)
    nc = _get_nc("wprep1a", build_wprep1a, 0)
    res = run_bass_kernel_spmd(nc, [
        {"wg_nat": np.ascontiguousarray(wg_n[704 * c:704 * (c + 1)]),
         "wu_nat": np.ascontiguousarray(wu_n[704 * c:704 * (c + 1)]),
         "wd_nat": np.ascontiguousarray(wd_n[256 * c:256 * (c + 1)])}
        for c in range(8)], core_ids=list(range(8)))
    wscg = np.concatenate([np.asarray(r["wscg_out"])[:, 0] for r in res.results])
    wscu = np.concatenate([np.asarray(r["wscu_out"])[:, 0] for r in res.results])
    wscd = np.concatenate([np.asarray(r["wscd_out"])[:, 0] for r in res.results])

    wgT_n = np.ascontiguousarray(wg_n.T)
    wuT_n = np.ascontiguousarray(wu_n.T)
    wdT_n = np.ascontiguousarray(wd_n.T)
    nc = _get_nc("wprep1b", build_wprep1b, 0)
    res = run_bass_kernel_spmd(nc, [
        {"wgT_sh": np.ascontiguousarray(wgT_n[256 * c:256 * (c + 1)]),
         "wuT_sh": np.ascontiguousarray(wuT_n[256 * c:256 * (c + 1)]),
         "wdT_sh": np.ascontiguousarray(wdT_n[704 * c:704 * (c + 1)]),
         "wscg_full": np.ascontiguousarray(wscg[None, :]),
         "wscu_full": np.ascontiguousarray(wscu[None, :]),
         "wscd_full": np.ascontiguousarray(wscd[None, :])}
        for c in range(8)], core_ids=list(range(8)))
    qwgT = np.ascontiguousarray(np.concatenate(
        [np.asarray(r["qwgT_sh"]) for r in res.results], axis=0))
    qwuT = np.ascontiguousarray(np.concatenate(
        [np.asarray(r["qwuT_sh"]) for r in res.results], axis=0))
    qwdT = np.ascontiguousarray(np.concatenate(
        [np.asarray(r["qwdT_sh"]) for r in res.results], axis=0))

    sig = np.ascontiguousarray(np.asarray(L1_ffn_down_reorder_index).astype(np.int32))
    nc = _get_nc("ffn1", build_ffn1, 0)
    res = run_bass_kernel_spmd(nc, [
        {"x_in": x3[c], "qwg_in": qwgT, "qwu_in": qwuT, "qwd_in": qwdT,
         "sig_in": sig} for c in range(8)], core_ids=list(range(8)))
    x4 = [np.asarray(r["y_out"]) for r in res.results]
    return _uncarve(x4)


def build_launch2(core):
    ctx = new_ctx()
    nc = ctx.nc
    x_in = nc.dram_tensor("x_in", [TPC, D], F32, kind="ExternalInput").ap()
    qf_in = nc.dram_tensor("qf_in", [2048, TPC], BF16, kind="ExternalInput").ap()
    kf_in = nc.dram_tensor("kf_in", [128, S], BF16, kind="ExternalInput").ap()
    vf_in = nc.dram_tensor("vf_in", [128, S], BF16, kind="ExternalInput").ap()
    wot_in = nc.dram_tensor("wot_in", [D, D], F32, kind="ExternalInput").ap()
    I = 5632
    NC_ = I // 128
    qwg_in = nc.dram_tensor("qwg_in", [D, I], BF16, kind="ExternalInput").ap()
    qwu_in = nc.dram_tensor("qwu_in", [D, I], BF16, kind="ExternalInput").ap()
    qwd_in = nc.dram_tensor("qwd_in", [I, D], BF16, kind="ExternalInput").ap()
    sig_in = nc.dram_tensor("sig_in", [I], I32, kind="ExternalInput").ap()
    y_out = nc.dram_tensor("y_out", [TPC, D], F32, kind="ExternalOutput").ap()
    w1T = nc.dram_tensor("w1T", [I, D], BF16, kind="Internal").ap()

    add_consts(ctx)
    onescol_f = ctx.sb("onescol_f", [128, 1], F32)
    onesrow_f = ctx.sb("onesrow_f", [1, 128], F32)
    ctx.op("vector", lambda e: e.memset(onescol_f[:], 1.0))
    ctx.op("vector", lambda e: e.memset(onesrow_f[:], 1.0))
    X = ctx.sb("X", [128, NT, D], F32)
    stat_m = ctx.sb("stat_m", [128, NT], F32)
    stat_r = ctx.sb("stat_r", [128, NT], F32)
    sq_scr = ctx.sb("sq_scr", [128, D], F32)
    ctx.push_scope()
    WoIn = ctx.sb("WoIn", [128, 16, TPC], BF16)
    ld_x = emit_load_x(ctx, X, x_in)
    for t in range(NT):
        emit_ln_stats(ctx, X, t, stat_m, stat_r, sq_scr,
                      wait_v=[("ld_x", ld_x)], wait_s=[("ld_x", ld_x)])
    dv_stats = ctx.now("dv")

    ctx.push_scope()
    k_aug = [ctx.sb(f"kaug{g}", [65, S], BF16) for g in range(2)]
    vfh = [ctx.sb(f"vfh{g}", [64, S], BF16) for g in range(2)]
    vtok = [ctx.sb(f"vtok{g}", [128, 16, 65], BF16) for g in range(2)]
    qa_t = ctx.sb("qa_t", [65, TPC], BF16)
    PT = ctx.sb("PT", [128, 16, 512], BF16)
    ksq = ctx.sb("ksq", [64, S], BF16)
    norm_sc = ctx.sb("norm_sc", [1, 2048], F32)
    qn_sc = ctx.sb("qn_sc", [1, TPC], F32)
    c_sc = ctx.sb("c_sc", [1, 4], F32)
    rs_sc = ctx.sb("rs_sc", [1, 512], F32)
    rbx = ctx.sb("rbx", [64, 512], F32)
    lnx = ctx.sb("lnx", [128, D], F32)

    ps_sc = [Bank(ctx.ps("ps_sc0", [128, 512], F32)), Bank(ctx.ps("ps_sc1", [128, 512], F32))]
    ps_av = [Bank(ctx.ps("ps_av0", [65, 512], F32)), Bank(ctx.ps("ps_av1", [65, 512], F32))]
    ps_bc = Bank(ctx.ps("ps_bc", [64, 512], F32))
    pt = [Bank(ctx.ps("pt0", [128, 1024], BF16)), Bank(ctx.ps("pt1", [128, 1024], BF16))]

    ctx.op("vector", lambda e: e.memset(k_aug[0][64:65, :], 1.0))
    ctx.op("vector", lambda e: e.memset(k_aug[1][64:65, :], 1.0))
    ctx.op("vector", lambda e: e.memset(vtok[0][:, :, 64:65], 1.0))
    dv_ones = ctx.op("vector", lambda e: e.memset(vtok[1][:, :, 64:65], 1.0))

    for g in range(2):
        ctx.op("sync", lambda e, g=g: e.dma_start(
            out=k_aug[g][0:64, :], in_=kf_in[g * 64:(g + 1) * 64, :]),
            inc="ld_kv", inc_n=16, dma=True)
    ctx.op("sync", lambda e: e.dma_start(out=vfh[0][:], in_=vf_in[0:64, :]), inc="ld_kv", inc_n=16, dma=True)
    ctx.op("sync", lambda e: e.dma_start(out=vfh[1][:], in_=vf_in[64:128, :]), inc="ld_kv", inc_n=16, dma=True)
    ld_kv = ctx.now("ld_kv")
    for g in range(2):
        for kt in range(16):
            bank = pt[kt % 2]
            w = [("ld_kv", ld_kv), ("ld_c", ctx.consts_ld), ("dv", dv_ones)] + bank.guard()
            pe_t = ctx.op("tensor", lambda e, g=g, kt=kt, bank=bank: e.transpose(
                bank[:, 0:64], vfh[g][:, kt * 128:(kt + 1) * 128],
                ctx.ident[0:64, 0:64]), wait=w)
            bank.last_evac = ("dv", ctx.op("vector", lambda e, g=g, kt=kt, bank=bank: e.tensor_copy(
                out=vtok[g][:, kt, 0:64], in_=bank[:, 0:64]), wait=[("pe", pe_t)]))
    dv_vt = ctx.now("dv")

    ac_pre = 0
    for g in range(2):
        acq = ctx.op("scalar", lambda e, g=g: e.activation(
            out=ksq[:], in_=k_aug[g][0:64, :], func=AF.Square), wait=[("ld_kv", ld_kv)])
        for blk in range(4):
            pe = ctx.op("tensor", lambda e, blk=blk: e.matmul(
                ps_sc[0][0:1, :], ctx.ones_col[0:64, :], ksq[:, blk * 512:(blk + 1) * 512],
                start=True, stop=True),
                wait=[("ac", acq)] + ps_sc[0].guard())
            ps_sc[0].last_evac = ("dv", ctx.op("vector", lambda e, blk=blk: e.tensor_copy(
                out=norm_sc[:, blk * 512:(blk + 1) * 512], in_=ps_sc[0][0:1, :]),
                wait=[("pe", pe)]))
        ctx.op("vector", lambda e, g=g: e.reduce_max(
            out=c_sc[:, g:g + 1], in_=norm_sc[:], axis=AX))
        acc = ctx.op("scalar", lambda e, g=g: e.activation(
            out=c_sc[:, g:g + 1], in_=c_sc[:, g:g + 1], func=AF.Sqrt),
            wait=[("dv", ctx.now("dv"))])
        ctx.op("vector", lambda e, g=g: e.tensor_scalar_mul(
            out=c_sc[:, 2 + g:3 + g], in0=c_sc[:, g:g + 1], scalar1=-1.0),
            wait=[("ac", acc)])
    dv_negc = ctx.now("dv")

    last_q_pe = 0
    prev = {}
    for h in range(32):
        g = h // 16
        lh = f"lq{h % 2}"
        ctx.op("sync", lambda e, h=h: e.dma_start(
            out=qa_t[0:64, :], in_=qf_in[h * 64:(h + 1) * 64, :]),
            wait=[("pe", last_q_pe)], inc=lh, inc_n=16, dma=True)
        ldq = ctx.now(lh)
        acq2 = ctx.op("scalar", lambda e: e.activation(
            out=ksq[:, 0:TPC], in_=qa_t[0:64, :], func=AF.Square),
            wait=[(lh, ldq), ("pe", prev.get("ksq_pe", 0))])
        for blk in range(2):
            pe = ctx.op("tensor", lambda e, blk=blk: e.matmul(
                ps_sc[0][0:1, :], ctx.ones_col[0:64, :], ksq[:, blk * 512:(blk + 1) * 512],
                start=True, stop=True),
                wait=[("ac", acq2)] + ps_sc[0].guard())
            ps_sc[0].last_evac = ("dv", ctx.op("vector", lambda e, blk=blk: e.tensor_copy(
                out=qn_sc[:, blk * 512:(blk + 1) * 512], in_=ps_sc[0][0:1, :]),
                wait=[("pe", pe)]))
        prev["ksq_pe"] = pe
        acq3 = ctx.op("scalar", lambda e: e.activation(
            out=qn_sc[:], in_=qn_sc[:], func=AF.Sqrt),
            wait=[ps_sc[0].last_evac])
        dvb = ctx.op("vector", lambda e, g=g: e.tensor_single_scalar(
            out=qa_t[64:65, :], in_=qn_sc[:], scalar=c_sc[:, 2 + g:3 + g], op=ALU.mult),
            wait=[("ac", acq3), ("dv", dv_negc)])
        for qb in range(2):
            av_bank = ps_av[qb]
            for kt in range(16):
                bank = ps_sc[kt % 2]
                w = bank.guard() + [("dv", dvb)]
                if kt == 0:
                    w.append(("pe", prev.get(f"avdone{qb}", 0)))
                pe = ctx.op("tensor", lambda e, g=g, kt=kt, qb=qb, bank=bank: e.matmul(
                    bank[:], k_aug[g][:, kt * 128:(kt + 1) * 128],
                    qa_t[:, qb * 512:(qb + 1) * 512], start=True, stop=True), wait=w)
                bank.last_evac = ("ac", ctx.op("scalar", lambda e, bank=bank, kt=kt: e.activation(
                    out=PT[:, kt, :], in_=bank[:], func=AF.Exp), wait=[("pe", pe)]))
            ac_exp = ctx.now("ac")
            for kt in range(16):
                w = []
                if kt == 0:
                    w = [("ac", ac_exp), ("dv", prev.get(f"avused{qb}", 0)), ("dv", dv_vt)]
                pe_av = ctx.op("tensor", lambda e, g=g, kt=kt, av_bank=av_bank: e.matmul(
                    av_bank[:], vtok[g][:, kt, :], PT[:, kt, :],
                    start=(kt == 0), stop=(kt == 15)), wait=w)
            prev[f"avdone{qb}"] = pe_av
            dvr = ctx.op("vector", lambda e, av_bank=av_bank: e.reciprocal(
                out=rs_sc[:], in_=av_bank[64:65, :]), wait=[("pe", pe_av)])
            pe_b = ctx.op("tensor", lambda e: e.matmul(
                ps_bc[:], onesrow_f[:, 0:64], rs_sc[:], start=True, stop=True),
                wait=[("dv", dvr)] + ps_bc.guard())
            ps_bc.last_evac = ("ac", ctx.op("scalar", lambda e: e.copy(
                out=rbx[:], in_=ps_bc[:]), wait=[("pe", pe_b)]))
            prev[f"avused{qb}"] = ctx.op("vector",
                lambda e, h=h, qb=qb, av_bank=av_bank: e.tensor_tensor(
                    out=WoIn[(h % 2) * 64:(h % 2) * 64 + 64, h // 2,
                             qb * 512:(qb + 1) * 512],
                    in0=av_bank[0:64, :], in1=rbx[:], op=ALU.mult),
                wait=[ps_bc.last_evac])
        last_q_pe = ctx.now("pe")
    dv_woin = ctx.now("dv")

    ctx.pop_scope()
    reuse_bar = ctx.barrier_wait()
    ctx.push_scope()
    wob = [ctx.sb(f"wob{i}", [128, DC, 512], BF16) for i in range(2)]
    osb = ctx.sb("osb", [128, 16, TPC], BF16)
    lnx2 = ctx.sb("lnx2", [128, D], F32)
    pw = [Bank(ctx.ps("pw0", [128, 512], F32)), Bank(ctx.ps("pw1", [128, 512], F32))]
    ptw = [Bank(ctx.ps("ptw0", [128, 1024], BF16)), Bank(ctx.ps("ptw1", [128, 1024], BF16))]
    wsrc = wot_in.rearrange("(c p) f -> p c f", p=128)
    last_wo_pe = {}
    for ob in range(4):
        wb = wob[ob % 2]
        wld = ctx.op("gpsimd", lambda e, ob=ob, wb=wb: e.dma_start(
            out=wb[:], in_=wsrc[:, :, ob * 512:(ob + 1) * 512]),
            wait=list(reuse_bar) + [("pe", last_wo_pe.get(ob % 2, 0))],
            inc=f"wl{ob % 2}", inc_n=16, dma=True)
        wldc = ctx.now(f"wl{ob % 2}")
        for fc in range(4):
            for qbh in range(2):
                bank = pw[(fc * 2 + qbh) % 2]
                w = list(bank.guard())
                if fc == 0 and qbh == 0:
                    w += [(f"wl{ob % 2}", wldc), ("dv", dv_woin)]
                for d in range(DC):
                    pe = ctx.op("tensor", lambda e, wb=wb, d=d, fc=fc, qbh=qbh, bank=bank:
                                e.matmul(bank[:], wb[:, d, fc * 128:(fc + 1) * 128],
                                         WoIn[:, d, qbh * 512:(qbh + 1) * 512],
                                         start=(d == 0), stop=(d == DC - 1)),
                                wait=w if d == 0 else ())
                bank.last_evac = ("ac", ctx.op("scalar",
                    lambda e, bank=bank, ob=ob, fc=fc, qbh=qbh: e.copy(
                        out=osb[:, ob * 4 + fc, qbh * 512:(qbh + 1) * 512], in_=bank[:]),
                    wait=[("pe", pe)]))
        last_wo_pe[ob % 2] = ctx.now("pe")
    ac_osb = ctx.now("ac")

    for t in range(NT):
        dvl = ctx.op("vector", lambda e, t=t: e.tensor_scalar(
            out=lnx2[:], in0=X[:, t, :], scalar1=stat_m[:, t:t + 1],
            scalar2=stat_r[:, t:t + 1], op0=ALU.subtract, op1=ALU.mult),
            wait=[("dv", dv_stats)])
        ctx.op("vector", lambda e, t=t: e.tensor_tensor(
            out=X[:, t, :], in0=X[:, t, :], in1=lnx2[:], op=ALU.add))
        for gg in range(2):
            bank = ptw[gg % 2]
            w = [("ac", ac_osb)] + bank.guard()
            for j in range(8):
                d = gg * 8 + j
                pe = ctx.op("tensor", lambda e, d=d, j=j, t=t, bank=bank: e.transpose(
                    bank[:, j * 128:(j + 1) * 128], osb[:, d, t * 128:(t + 1) * 128],
                    ctx.ident[:]), wait=w if j == 0 else ())
            bank.last_evac = ("dv", ctx.op("vector", lambda e, t=t, gg=gg, bank=bank: e.tensor_tensor(
                out=X[:, t, gg * 1024:(gg + 1) * 1024],
                in0=X[:, t, gg * 1024:(gg + 1) * 1024],
                in1=bank[:], op=ALU.add), wait=[("pe", pe)]))
    ctx.pop_scope()
    ctx.pop_scope()
    bar_attn1 = ctx.barrier_wait()


    HT = ctx.sb("HT", [128, DC, TPC], BF16)
    sig1 = ctx.sb("sig1", [128, NT], F32)
    sig2 = ctx.sb("sig2", [128, NT], F32)
    s448 = ctx.sb("s448", [128, NT], F32)

    scat_t = ctx.sb("f1scat_t", [128, D], BF16)
    sig_sb = ctx.sb("f1sig_sb", [128, 1], I32)
    for c in range(NC_):
        lsem = f"sc{c % 2}"
        lc = ctx.op("sync", lambda e, c=c: e.dma_start(
            out=scat_t[:], in_=qwd_in[c * 128:(c + 1) * 128, :]),
            wait=(list(bar_attn1) if c == 0 else []) + [("scat", ctx.now("scat"))],
            inc=lsem, inc_n=16, dma=True)
        lc2 = ctx.op("sync", lambda e, c=c: e.dma_start(
            out=sig_sb[:], in_=sig_in[c * 128:(c + 1) * 128, None]),
            inc=lsem, inc_n=16, dma=True)
        ctx.op("gpsimd", lambda e, c=c: e.indirect_dma_start(
            out=w1T, out_offset=IndirectOffsetOnAxis(sig_sb[:], 0),
            in_=scat_t[:], in_offset=None),
            wait=[(lsem, lc2)], inc="scat", inc_n=16, dma=True)
    scat_done = ctx.now("scat")

    ctx.push_scope()
    QB = 1024
    qs_t = {k: ctx.sb(f"f1qs_{k}", [128, QB], F32 if k != "ian" else I32)
            for k in ("a", "ian", "fan", "rr", "ef", "ui", "t1")}
    xm = ctx.sb("f1xm", [128, D], F32)
    qxb = ctx.sb("f1qxb", [128, D], BF16)
    pt = [Bank(ctx.ps("f1pt0", [128, 1024], BF16)), Bank(ctx.ps("f1pt1", [128, 1024], BF16))]
    ctx.wait("vector", [("ld_c", ctx.consts_ld)])
    for t in range(NT):
        emit_ln_stats(ctx, X, t, stat_m, stat_r, sq_scr,
                      wait_v=list(bar_attn1), wait_s=list(bar_attn1))
        dvxm = ctx.op("vector", lambda e, t=t: e.tensor_single_scalar(
            out=xm[:], in_=X[:, t, :], scalar=stat_m[:, t:t + 1], op=ALU.subtract))
        dvm = ctx.op("vector", lambda e, t=t: e.reduce_max(
            out=s448[:, t:t + 1], in_=xm[:], axis=AX, apply_absolute_value=True))
        ctx.op("vector", lambda e, t=t: e.scalar_tensor_tensor(
            out=sig1[:, t:t + 1], in0=s448[:, t:t + 1], scalar=1.0 / 448.0,
            in1=stat_r[:, t:t + 1], op0=ALU.mult, op1=ALU.mult))
        ctx.op("vector", lambda e, t=t: e.reciprocal(
            out=s448[:, t:t + 1], in_=s448[:, t:t + 1]))
        dvs = ctx.op("vector", lambda e, t=t: e.tensor_scalar_mul(
            out=s448[:, t:t + 1], in0=s448[:, t:t + 1], scalar1=448.0))
        for j in range(0, D, QB):
            dvw = ctx.op("vector", lambda e, t=t, j=j: e.tensor_single_scalar(
                out=xm[:, j:j + QB], in_=xm[:, j:j + QB], scalar=s448[:, t:t + 1],
                op=ALU.mult), wait=[("dv", dvs)])
            qv = {k: qs_t[k][:] for k in qs_t}
            emit_fp8_quant(ctx, xm[:, j:j + QB], qxb[:, j:j + QB], qv,
                           wait=[("dv", dvw)])
        emit_transpose_tile(ctx, qxb[:], HT, t, pt, wait_pe=[("dv", ctx.now("dv"))])
    dv_qxt = ctx.now("dv")
    ctx.pop_scope()
    barA = ctx.barrier_wait()

    ctx.push_scope()
    OB = 256
    NOB = I // OB
    qs_t = {k: ctx.sb(f"q2_{k}", [128, QB], F32 if k != "ian" else I32)
            for k in ("a", "ian", "fan", "rr", "ef", "ui", "t1")}
    wgb_t = ctx.sb("wgb", [128, DC, OB], BF16)
    wub_t = ctx.sb("wub", [128, DC, OB], BF16)
    wgb = [wgb_t, wgb_t]
    wub = [wub_t, wub_t]
    Tq = ctx.sb("Tq", [128, 2, I], BF16)
    qtb = ctx.sb("qtb", [128, QB], BF16)
    qtT = ctx.sb("qtT", [128, NC_, 256], BF16)
    w1c = [ctx.sb(f"w1c{i}", [128, 1024], BF16) for i in range(2)]
    gsb = ctx.sb("gsb", [128, OB], F32)
    usb = ctx.sb("usb", [128, OB], F32)
    rmx = ctx.sb("rmx", [128, 1], F32)
    psg = Bank(ctx.ps("psg", [128, 512], F32))
    psu = Bank(ctx.ps("psu", [128, 512], F32))
    ptq = [Bank(ctx.ps("ptq0", [128, 1024], BF16)), Bank(ctx.ps("ptq1", [128, 1024], BF16))]
    psd = [Bank(ctx.ps(f"psd{i}", [128, 512], F32)) for i in range(4)]
    wg_src = qwg_in.rearrange("(c p) f -> p c f", p=128)
    wu_src = qwu_in.rearrange("(c p) f -> p c f", p=128)
    last_gu_pe = {}
    w1_use = {}
    for quarter in range(4):
        tts = [quarter * 2, quarter * 2 + 1]
        for ob in range(NOB):
            lw = f"gw{ob % 2}"
            wld = ctx.op("gpsimd", lambda e, ob=ob: e.dma_start(
                out=wgb[ob % 2][:], in_=wg_src[:, :, ob * OB:(ob + 1) * OB]),
                wait=(list(barA) if (quarter == 0 and ob < 2) else []) +
                     [("pe", last_gu_pe.get("w", 0))],
                inc=lw, inc_n=16, dma=True)
            ctx.op("gpsimd", lambda e, ob=ob: e.dma_start(
                out=wub[ob % 2][:], in_=wu_src[:, :, ob * OB:(ob + 1) * OB]),
                inc=lw, inc_n=16, dma=True)
            lwc = ctx.now(lw)
            for ti, tt in enumerate(tts):
                wmm = [(lw, lwc), ("dv", dv_qxt)] + psg.guard() + psu.guard()
                for d in range(DC):
                    ctx.op("tensor", lambda e, d=d, tt=tt, ob=ob: e.matmul(
                        psg[:, 0:OB], HT[:, d, tt * 128:(tt + 1) * 128], wgb[ob % 2][:, d, :],
                        start=(d == 0), stop=(d == DC - 1)),
                        wait=wmm if d == 0 else ())
                    pe_mm = ctx.op("tensor", lambda e, d=d, tt=tt, ob=ob: e.matmul(
                        psu[:, 0:OB], HT[:, d, tt * 128:(tt + 1) * 128], wub[ob % 2][:, d, :],
                        start=(d == 0), stop=(d == DC - 1)))
                psg.last_evac = ("ac", ctx.op("scalar", lambda e, tt=tt: e.activation(
                    out=gsb[:], in_=psg[:, 0:OB], func=AF.Silu, scale=sig1[:, tt:tt + 1]),
                    wait=[("pe", pe_mm), ("dv", last_gu_pe.get("tq", 0))]))
                psu.last_evac = ("dv", ctx.op("vector", lambda e, tt=tt: e.tensor_single_scalar(
                    out=usb[:], in_=psu[:, 0:OB], scalar=sig1[:, tt:tt + 1], op=ALU.mult),
                    wait=[("pe", pe_mm)]))
                last_gu_pe["tq"] = ctx.op("vector", lambda e, ti=ti, ob=ob: e.tensor_tensor(
                    out=Tq[:, ti, ob * OB:(ob + 1) * OB], in0=gsb[:], in1=usb[:],
                    op=ALU.mult), wait=[psg.last_evac])
            last_gu_pe["w"] = ctx.now("pe")
        for ti, tt in enumerate(tts):
            for cb in range(0, I, QB):
                wj = min(QB, I - cb)
                dvr = ctx.op("vector", lambda e, ti=ti, cb=cb, wj=wj: e.reduce_max(
                    out=qs_t["a"][:, 0:1], in_=Tq[:, ti, cb:cb + wj], axis=AX,
                    apply_absolute_value=True))
                if cb == 0:
                    ctx.op("vector", lambda e: e.tensor_copy(out=rmx[:], in_=qs_t["a"][:, 0:1]))
                else:
                    ctx.op("vector", lambda e: e.tensor_tensor(
                        out=rmx[:], in0=rmx[:], in1=qs_t["a"][:, 0:1], op=ALU.max))
            ctx.op("vector", lambda e, tt=tt: e.tensor_scalar_mul(
                out=sig2[:, tt:tt + 1], in0=rmx[:], scalar1=1.0 / 448.0))
            ctx.op("vector", lambda e: e.reciprocal(out=rmx[:], in_=rmx[:]))
            dvs2 = ctx.op("vector", lambda e: e.tensor_scalar_mul(
                out=rmx[:], in0=rmx[:], scalar1=448.0))
            for cb in range(0, I, QB):
                wj = min(QB, I - cb)
                dvw = ctx.op("vector", lambda e, ti=ti, cb=cb, wj=wj: e.tensor_single_scalar(
                    out=qs_t["t1"][:, 0:wj], in_=Tq[:, ti, cb:cb + wj], scalar=rmx[:],
                    op=ALU.mult), wait=[("dv", dvs2)])
                qv = {k: qs_t[k][:, 0:wj] for k in qs_t}
                dvq = emit_fp8_quant(ctx, qs_t["t1"][:, 0:wj], qtb[:, 0:wj], qv,
                                     wait=[("dv", dvw)])
                nch = wj // 128
                for hb in range(0, nch, 8):
                    bank = ptq[(hb // 8) % 2]
                    nd = min(8, nch - hb)
                    wtr = [("dv", dvq)] + bank.guard()
                    for j in range(nd):
                        pe_t = ctx.op("tensor", lambda e, j=j, hb=hb, bank=bank: e.transpose(
                            bank[:, j * 128:(j + 1) * 128],
                            qtb[:, (hb + j) * 128:(hb + j + 1) * 128], ctx.ident[:]),
                            wait=wtr if j == 0 else ())
                    bank.last_evac = ("dv", ctx.op("vector",
                        lambda e, bank=bank, cb=cb, hb=hb, nd=nd, ti=ti: e.tensor_copy(
                            out=qtT[:, cb // 128 + hb:cb // 128 + hb + nd,
                                    ti * 128:(ti + 1) * 128],
                            in_=bank[:, 0:nd * 128].rearrange("p (j c) -> p j c", c=128)),
                        wait=[("pe", pe_t)]))
        dv_qtT = ctx.now("dv")
        for ohalf in range(2):
            for c in range(NC_):
                lw1 = f"dw{c % 2}"
                wfirst = [("scat", scat_done)] if (quarter == 0 and ohalf == 0 and c < 2) else []
                ctx.op("gpsimd", lambda e, c=c, ohalf=ohalf: e.dma_start(
                    out=w1c[c % 2][:], in_=w1T[c * 128:(c + 1) * 128,
                                               ohalf * 1024:(ohalf + 1) * 1024]),
                    wait=wfirst + [("pe", w1_use.get(c % 2, 0))], inc=lw1, inc_n=16, dma=True)
                lw1c = ctx.now(lw1)
                for ti in range(2):
                    for j in range(2):
                        wmm = []
                        if ti == 0 and j == 0:
                            wmm = [(lw1, lw1c), ("dv", dv_qtT)]
                            if c == 0:
                                for b in psd:
                                    wmm += b.guard()
                        pe_d = ctx.op("tensor", lambda e, c=c, ti=ti, j=j, ohalf=ohalf:
                                      e.matmul(psd[ti * 2 + j][:],
                                               qtT[:, c, ti * 128:(ti + 1) * 128],
                                               w1c[c % 2][:, j * 512:(j + 1) * 512],
                                               start=(c == 0), stop=(c == NC_ - 1)),
                                      wait=wmm)
                w1_use[c % 2] = ctx.now("pe")
            for ti, tt in enumerate(tts):
                for j in range(2):
                    psd[ti * 2 + j].last_evac = ("dv", ctx.op("vector",
                        lambda e, tt=tt, ti=ti, j=j, ohalf=ohalf: e.scalar_tensor_tensor(
                            out=X[:, tt, ohalf * 1024 + j * 512:ohalf * 1024 + (j + 1) * 512],
                            in0=psd[ti * 2 + j][:], scalar=sig2[:, tt:tt + 1],
                            in1=X[:, tt, ohalf * 1024 + j * 512:ohalf * 1024 + (j + 1) * 512],
                            op0=ALU.mult, op1=ALU.add),
                        wait=[("pe", ctx.now("pe"))] if (ti == 0 and j == 0) else ()))
        for tt in tts:
            ctx.op("sync", lambda e, tt=tt: e.dma_start(
                out=y_out.rearrange("(t p) d -> p t d", p=128)[:, tt, :], in_=X[:, tt, :]),
                wait=[("dv", ctx.now("dv"))], inc="st_out", inc_n=16, dma=True)
    ctx.wait("sync", [("st_out", ctx.now("st_out"))])
    ctx.build_and_close()
    return ctx.nc


def emit_ln_h_transpose(ctx, X, nt, stat_m, stat_r, sq_scr, h_bf, HT, pt,
                        wait_ld, l1cs=None):
    for t in range(nt):
        emit_ln_stats(ctx, X, t, stat_m, stat_r, sq_scr,
                      wait_v=list(wait_ld), wait_s=list(wait_ld))
        hb = h_bf[t % 2]
        if l1cs is None:
            hdv = emit_h_tile(ctx, X, t, stat_m, stat_r, hb)
        else:
            csb, rho_scr, h_f32, cs_ac = l1cs
            m = stat_m[:, t:t + 1]
            r = stat_r[:, t:t + 1]
            sc = rho_scr[:]
            ctx.op("vector", lambda e, r=r, sc=sc: e.tensor_tensor(
                out=sc, in0=r, in1=r, op=ALU.mult))
            dvp = ctx.op("vector", lambda e, sc=sc: e.tensor_scalar(
                out=sc, in0=sc, scalar1=-1e-5, scalar2=1.0 + 1e-8,
                op0=ALU.mult, op1=ALU.add))
            acs = ctx.op("scalar", lambda e, sc=sc: e.activation(
                out=sc, in_=sc, func=AF.Sqrt), wait=[("dv", dvp)])
            ctx.op("vector", lambda e, sc=sc: e.reciprocal(out=sc, in_=sc),
                   wait=[("ac", acs)])
            ctx.op("vector", lambda e, sc=sc, r=r: e.tensor_tensor(
                out=sc, in0=sc, in1=r, op=ALU.mult))
            ctx.op("vector", lambda e, t=t, m=m, sc=sc: e.tensor_scalar(
                out=h_f32[:], in0=X[:, t, :], scalar1=m, scalar2=sc,
                op0=ALU.subtract, op1=ALU.mult))
            hdv = ctx.op("vector", lambda e, hb=hb: e.tensor_tensor(
                out=hb[:], in0=h_f32[:], in1=csb[:], op=ALU.mult),
                wait=[("ac", cs_ac)])
        emit_transpose_tile(ctx, hb[:], HT, t, pt, wait_pe=[("dv", hdv)])
    return ctx.now("dv")


def emit_proj_stream(ctx, HT, nt, wt_src_view, col0, col1, wblk, pq, sink_fn,
                     wait_ready, sem_prefix, FB=768):
    nq = nt * 128 // 512
    assert (col1 - col0) % FB == 0, (col0, col1, FB)
    nfb = (col1 - col0) // FB
    last_blk = {}
    for fb in range(nfb):
        wb = wblk[fb % 2]
        wsem = f"{sem_prefix}{fb % 2}"
        buf_key = fb % 2 if wblk[0] is not wblk[1] else "w"
        wld = ctx.op("gpsimd", lambda e, fb=fb, wb=wb: e.dma_start(
            out=wb[:, :, 0:FB], in_=wt_src_view[:, :, col0 + fb * FB:col0 + (fb + 1) * FB]),
            wait=list(wait_ready) + [("pe", last_blk.get(buf_key, 0))],
            inc=wsem, inc_n=16, dma=True)
        wldc = ctx.now(wsem)
        for fc in range(FB // 128):
            f_abs = col0 + fb * FB + fc * 128
            bks = [pq[(fc * nq + j) % len(pq)] for j in range(nq)]
            for d in range(DC):
                for qb in range(nq):
                    w = []
                    if d == 0:
                        w = list(bks[qb].guard())
                        if qb == 0:
                            w += [(wsem, wldc)]
                    pe = ctx.op("tensor", lambda e, wb=wb, d=d, fc=fc, qb=qb, bks=bks:
                                e.matmul(bks[qb][:], wb[:, d, fc * 128:(fc + 1) * 128],
                                         HT[:, d, qb * 512:(qb + 1) * 512],
                                         start=(d == 0), stop=(d == DC - 1)),
                                wait=w)
            sink_fn(ctx, f_abs, bks, pe)
        last_blk[buf_key] = ctx.now("pe")


def build_launch1():
    ctx = new_ctx()
    nc = ctx.nc
    H0 = 8192
    I = 5632
    x_in = nc.dram_tensor("x_in", [TPC, D], F32, kind="ExternalInput").ap()
    amask_in = nc.dram_tensor("amask_in", [128, 2], F32, kind="ExternalInput").ap()
    xf_in = nc.dram_tensor("xf_in", [S, D], F32, kind="ExternalInput").ap()
    wqkvT_in = nc.dram_tensor("wqkvT_in", [D, 6144], F32, kind="ExternalInput").ap()
    lam_in = nc.dram_tensor("lam_in", [4, 128], F32, kind="ExternalInput").ap()
    gnt_in = nc.dram_tensor("gnt_in", [128, 32], F32, kind="ExternalInput").ap()
    wot_in = nc.dram_tensor("wot_in", [D, D], F32, kind="ExternalInput").ap()
    wgt_in = nc.dram_tensor("wgt_in", [D, H0], F32, kind="ExternalInput").ap()
    wut_in = nc.dram_tensor("wut_in", [D, H0], F32, kind="ExternalInput").ap()
    wdt_in = nc.dram_tensor("wdt_in", [H0, D], F32, kind="ExternalInput").ap()
    wcatT_in = nc.dram_tensor("wcatT_in", [D, 2304], F32, kind="ExternalInput").ap()
    ns_in = nc.dram_tensor("ns_in", [1, D], F32, kind="ExternalInput").ap()
    ss_in = nc.dram_tensor("ss_in", [1, D], F32, kind="ExternalInput").ap()
    wg_nat = nc.dram_tensor("wg_nat", [I, D], F32, kind="ExternalInput").ap()
    wu_nat = nc.dram_tensor("wu_nat", [I, D], F32, kind="ExternalInput").ap()
    wd_nat = nc.dram_tensor("wd_nat", [D, I], F32, kind="ExternalInput").ap()
    wgT_sh = nc.dram_tensor("wgT_sh", [256, I], F32, kind="ExternalInput").ap()
    wuT_sh = nc.dram_tensor("wuT_sh", [256, I], F32, kind="ExternalInput").ap()
    wdT_sh = nc.dram_tensor("wdT_sh", [704, D], F32, kind="ExternalInput").ap()

    x2_out = nc.dram_tensor("x2_out", [TPC, D], F32, kind="ExternalOutput").ap()
    q1f_out = nc.dram_tensor("q1f_out", [2048, TPC], BF16, kind="ExternalOutput").ap()
    kv1_out = nc.dram_tensor("kv1_out", [256, TPC], BF16, kind="ExternalOutput").ap()
    qwgT_out = nc.dram_tensor("qwgT_sh", [256, I], BF16, kind="ExternalOutput").ap()
    qwuT_out = nc.dram_tensor("qwuT_sh", [256, I], BF16, kind="ExternalOutput").ap()
    qwdT_out = nc.dram_tensor("qwdT_sh", [704, D], BF16, kind="ExternalOutput").ap()

    qf_dram = nc.dram_tensor("qf_dram", [2048, TPC], BF16, kind="Internal").ap()
    kvf_dram = nc.dram_tensor("kvf_dram", [4096, S], BF16, kind="Internal").ap()
    pT_dram = nc.dram_tensor("pT_dram", [H0, TPC], BF16, kind="Internal").ap()
    wsc_dram = nc.dram_tensor("wsc_dram", [2 * I + D], F32, kind="Internal").ap()

    add_consts(ctx)
    X = ctx.sb("X", [128, NT, D], F32)
    stat_m = ctx.sb("stat_m", [128, NT], F32)
    stat_r = ctx.sb("stat_r", [128, NT], F32)
    ld_x = emit_load_x(ctx, X, x_in)

    ctx.push_scope()
    sq_scr = ctx.sb("sq_scr", [128, D], F32)
    HT = ctx.sb("HT", [128, DC, TPC], BF16)
    h_bf = [ctx.sb(f"h_bf{i}", [128, D], BF16) for i in range(2)]
    pt = [Bank(ctx.ps(f"pt{i}", [128, 1024], BF16)) for i in range(2)]
    wblk = [ctx.sb(f"wblk{i}", [128, DC, 1024], BF16) for i in range(2)]
    stg = [ctx.sb(f"stg{i}", [128, TPC], BF16) for i in range(2)]
    pq = [Bank(ctx.ps(f"pq{i}", [128, 512], F32)) for i in range(4)]
    ctx.wait("vector", [("ld_c", ctx.consts_ld)])
    emit_ln_h_transpose(ctx, X, NT, stat_m, stat_r, sq_scr, h_bf, HT, pt,
                        [("ld_x", ld_x)])
    wsrc = wqkvT_in.rearrange("(c p) f -> p c f", p=128)
    st_track = {}

    def q_sink(ctx, f_abs, bks, pe, dram=qf_dram, scale=0.125, base=0):
        sg = stg[(f_abs // 128) % 2]
        key = ("s", (f_abs // 128) % 2)
        w = [("pe", pe)]
        if st_track.get(key):
            w.append(st_track[key])
        a1 = ctx.op("scalar", lambda e, sg=sg, b=bks[0]: e.activation(
            out=sg[:, 0:512], in_=b[:], func=AF.Copy, scale=scale), wait=w)
        a2 = ctx.op("scalar", lambda e, sg=sg, b=bks[1]: e.activation(
            out=sg[:, 512:1024], in_=b[:], func=AF.Copy, scale=scale))
        bks[0].last_evac = ("ac", a1)
        bks[1].last_evac = ("ac", a2)
        ssem = f"stq{(f_abs // 128) % 2}"
        stc = ctx.op("sync", lambda e, sg=sg, f_abs=f_abs, dram=dram, base=base:
                     e.dma_start(out=dram[f_abs - base:f_abs - base + 128, :], in_=sg[:]),
                     wait=[("ac", a2)], inc=ssem, inc_n=16, dma=True)
        st_track[key] = (ssem, stc)

    emit_proj_stream(ctx, HT, NT, wsrc, 0, 2048, wblk, pq, q_sink,
                     [("dv", ctx.now("dv"))], "wq", FB=1024)
    q_done = [(f"stq{i}", ctx.now(f"stq{i}")) for i in range(2)]

    Xf = ctx.sb("Xf", [128, 2, D], F32)
    for half in range(2):
        for t in range(NT):
            lsem = f"ld_xf{t % 2}"
            lh = ctx.op("sync", lambda e, half=half, t=t: e.dma_start(
                out=Xf[:, t % 2, :],
                in_=xf_in[half * 1024 + t * 128:half * 1024 + (t + 1) * 128, :]),
                wait=[("dv", ctx.now("dv")), ("ac", ctx.now("ac"))],
                inc=lsem, inc_n=16, dma=True)
            lhc = ctx.now(lsem)
            emit_ln_stats(ctx, Xf.rearrange("p a d -> p a d") if False else Xf,
                          t % 2, stat_m, stat_r, sq_scr,
                          wait_v=[(lsem, lhc)], wait_s=[(lsem, lhc)])
            hb = h_bf[t % 2]
            hdv = emit_h_tile(ctx, Xf, t % 2, stat_m, stat_r, hb)
            emit_transpose_tile(ctx, hb[:], HT, t, pt, wait_pe=[("dv", hdv)])

        def kv_sink(ctx, f_abs, bks, pe, half=half):
            sg = stg[(f_abs // 128) % 2]
            key = ("s", (f_abs // 128) % 2)
            w = [("pe", pe)]
            if st_track.get(key):
                w.append(st_track[key])
            a1 = ctx.op("scalar", lambda e, sg=sg, b=bks[0]: e.copy(
                out=sg[:, 0:512], in_=b[:]), wait=w)
            a2 = ctx.op("scalar", lambda e, sg=sg, b=bks[1]: e.copy(
                out=sg[:, 512:1024], in_=b[:]))
            bks[0].last_evac = ("ac", a1)
            bks[1].last_evac = ("ac", a2)
            ssem = f"stq{(f_abs // 128) % 2}"
            stc = ctx.op("sync", lambda e, sg=sg, f_abs=f_abs, half=half:
                         e.dma_start(out=kvf_dram[f_abs - 2048:f_abs - 2048 + 128,
                                                  half * 1024:(half + 1) * 1024],
                                     in_=sg[:]),
                         wait=[("ac", a2)], inc=ssem, inc_n=16, dma=True)
            st_track[key] = (ssem, stc)

        emit_proj_stream(ctx, HT, NT, wsrc, 2048, 6144, wblk, pq, kv_sink,
                         [("dv", ctx.now("dv"))], f"wkv{half}", FB=1024)
    kv_done = [(f"stq{i}", ctx.now(f"stq{i}")) for i in range(2)]
    ctx.pop_scope()
    bar_q0 = ctx.barrier_wait()
    ctx._l1_bar = bar_q0
    ctx._l1_sources = (qf_dram, kvf_dram, q_done, kv_done)
    return ctx, nc, dict(x_in=x_in, lam_in=lam_in, gnt_in=gnt_in, wot_in=wot_in,
                         wgt_in=wgt_in, wut_in=wut_in, wdt_in=wdt_in,
                         wcatT_in=wcatT_in, ns_in=ns_in, ss_in=ss_in,
                         wg_nat=wg_nat, wu_nat=wu_nat, wd_nat=wd_nat,
                         wgT_sh=wgT_sh, wuT_sh=wuT_sh, wdT_sh=wdT_sh,
                         x2_out=x2_out, q1f_out=q1f_out, kv1_out=kv1_out,
                         qwgT_out=qwgT_out, qwuT_out=qwuT_out, qwdT_out=qwdT_out,
                         pT_dram=pT_dram, wsc_dram=wsc_dram, X=X,
                         stat_m=stat_m, stat_r=stat_r, amask_in=amask_in)


def emit_attn0_part(ctx, nc, io, qf_dram, kvf_dram, q_done, kv_done):
    x_in = io["x_in"]
    lam_in = io["lam_in"]
    gnt_in = io["gnt_in"]
    wot_in = io["wot_in"]
    amask_in = io["amask_in"]
    X = io["X"]
    ld_x = 16
    onescol_f = ctx.sb("a0onescol_f", [128, 1], F32)
    onesrow_f = ctx.sb("a0onesrow_f", [1, 128], F32)
    ctx.op("vector", lambda e: e.memset(onescol_f[:], 1.0))
    ctx.op("vector", lambda e: e.memset(onesrow_f[:], 1.0))
    amask = ctx.sb("amask", [128, 2], F32)
    ctx.op("sync", lambda e: e.dma_start(out=amask[:], in_=amask_in),
           inc="ld_lam", inc_n=16, dma=True)
    lamt = ctx.sb("lamt", [128, 6], F32)
    for i in range(4):
        ctx.op("sync", lambda e, i=i: e.dma_start(
            out=lamt[:, i:i + 1], in_=lam_in[i, :, None]), inc="ld_lam", inc_n=16, dma=True)
    gnt = ctx.sb("gnt", [128, 32], F32)
    ctx.op("sync", lambda e: e.dma_start(out=gnt[:], in_=gnt_in),
           inc="ld_lam", inc_n=16, dma=True)
    llam = ctx.now("ld_lam")
    ctx.op("vector", lambda e: e.tensor_tensor(
        out=lamt[:, 4:5], in0=lamt[:, 0:1], in1=lamt[:, 1:2], op=ALU.mult),
        wait=[("ld_lam", llam)])
    ctx.op("vector", lambda e: e.tensor_tensor(
        out=lamt[:, 5:6], in0=lamt[:, 2:3], in1=lamt[:, 3:4], op=ALU.mult))
    ctx.op("scalar", lambda e: e.activation(out=lamt[:, 4:5], in_=lamt[:, 4:5], func=AF.Exp),
           wait=[("dv", ctx.now("dv"))])
    a_e2 = ctx.op("scalar", lambda e: e.activation(out=lamt[:, 5:6], in_=lamt[:, 5:6], func=AF.Exp))
    ctx.op("vector", lambda e: e.tensor_tensor(
        out=lamt[:, 4:5], in0=lamt[:, 4:5], in1=lamt[:, 5:6], op=ALU.subtract),
        wait=[("ac", a_e2)])
    ctx.op("vector", lambda e: e.tensor_scalar_add(
        out=lamt[:, 4:5], in0=lamt[:, 4:5], scalar1=LAMBDA_INIT))
    dv_lam = ctx.op("vector", lambda e: e.tensor_scalar_mul(
        out=lamt[:, 5:6], in0=lamt[:, 4:5], scalar1=-1.0))
    ac_gn = ctx.op("scalar", lambda e: e.activation(
        out=gnt[:], in_=gnt[:], func=AF.Copy, scale=1.0 - LAMBDA_INIT),
        wait=[("ld_lam", llam)])


    ctx.push_scope()
    WoIn = ctx.sb("WoIn", [128, 16, TPC], BF16)
    ctx.push_scope()
    q1a = ctx.sb("q1a", [65, TPC], BF16)
    q2a = ctx.sb("q2a", [65, TPC], BF16)
    k1a = ctx.sb("k1a", [65, 1536], BF16)
    k2a = ctx.sb("k2a", [65, 1536], BF16)
    vh = ctx.sb("vh", [128, 1536], BF16)
    kd1 = ctx.sb("kd1", [65, 2, 512], BF16)
    kd2 = ctx.sb("kd2", [65, 2, 512], BF16)
    vd = ctx.sb("vd", [128, 2, 512], BF16)
    vtok = ctx.sb("vtok", [128, 16, 128], BF16)
    PT1 = ctx.sb("PT1", [128, 16, 512], BF16)
    PT2 = ctx.sb("PT2", [128, 16, 512], BF16)
    ksq = ctx.sb("ksq", [64, 2560], BF16)
    norm_sc = ctx.sb("norm_sc", [1, 2560], F32)
    qn_sc = ctx.sb("qn_sc", [1, TPC], F32)
    c_sc = ctx.sb("c_sc", [1, 2], F32)
    rs_sc = ctx.sb("rs_sc", [1, 1024], F32)
    mrow = ctx.sb("mrow", [1, 512], F32)
    vrow = ctx.sb("vrow", [1, 512], F32)
    bc = ctx.sb("bc", [128, 4, 1024], F32)
    t1s = ctx.sb("t1s", [128, 512], F32)
    t2s = ctx.sb("t2s", [128, 512], F32)
    Acomb = ctx.sb("Acomb", [128, TPC], F32)
    Asq = ctx.sb("Asq", [128, TPC], F32)

    ps_sc = [Bank(ctx.ps("ps_sc0", [128, 512], F32)), Bank(ctx.ps("ps_sc1", [128, 512], F32))]
    ps_av1 = Bank(ctx.ps("ps_av1", [128, 512], F32))
    ps_av2 = Bank(ctx.ps("ps_av2", [128, 512], F32))
    ps_sum1 = Bank(ctx.ps("ps_sum1", [1, 512], F32))
    ps_sum2 = Bank(ctx.ps("ps_sum2", [1, 512], F32))
    pt = [Bank(ctx.ps("a0pt0", [128, 1024], BF16)), Bank(ctx.ps("a0pt1", [128, 1024], BF16))]

    ctx.op("vector", lambda e: e.memset(k1a[64:65, :], 1.0))
    ctx.op("vector", lambda e: e.memset(k2a[64:65, :], 1.0))
    ctx.op("vector", lambda e: e.memset(kd1[64:65, :, :], 1.0))
    ctx.op("vector", lambda e: e.memset(kd2[64:65, :, :], 1.0))
    dv_ones = ctx.now("dv")

    kvdiag = nc.dram_tensor("kvdiag_dram", [4096, 1024], BF16, kind="Internal").ap()
    regs = {}

    def _dregs(e):
        if "b0A" not in regs:
            pid = e.partition_id()
            par = e.scalar_reg_alu(ALU.mod, pid, 2)
            tmp = e.scalar_reg_alu(ALU.mult, par, -512)
            regs["b0A"] = e.scalar_reg_alu(ALU.add, tmp, 1536)
            regs["b0B"] = e.scalar_reg_alu(ALU.mult, par, 512)
        return regs

    for pos, key in ((0, "b0A"), (1, "b0B")):
        ctx.op("gpsimd", lambda e, pos=pos, key=key: e.dma_start(
            out=kvdiag[:, pos * 512:(pos + 1) * 512],
            in_=kvf_dram[:, bass.ds(_dregs(e)[key], 512)]),
            wait=(list(kv_done) if pos == 0 else ()), inc="kvd", inc_n=16, dma=True)
    kvd_done = ctx.now("kvd")

    last_head_pe = 0
    prev_gn = {}

    for h in range(16):
        lh = f"ldh{h % 2}"
        lds = f"ldd{h % 2}"
        w_guard = [("pe", last_head_pe)] + (q_done + kv_done if h == 0 else [])
        ctx.op("sync", lambda e, h=h: e.dma_start(
            out=q1a[0:64, :], in_=qf_dram[h * 128:h * 128 + 64, :]),
            wait=w_guard, inc=lh, inc_n=16, dma=True)
        ctx.op("sync", lambda e, h=h: e.dma_start(
            out=q2a[0:64, :], in_=qf_dram[h * 128 + 64:h * 128 + 128, :]),
            inc=lh, inc_n=16, dma=True)
        ctx.op("sync", lambda e, h=h: e.dma_start(
            out=k1a[0:64, :], in_=kvf_dram[h * 128:h * 128 + 64, 0:1536]),
            inc=lh, inc_n=16, dma=True)
        ctx.op("sync", lambda e, h=h: e.dma_start(
            out=k2a[0:64, :], in_=kvf_dram[h * 128 + 64:h * 128 + 128, 0:1536]),
            inc=lh, inc_n=16, dma=True)
        ctx.op("sync", lambda e, h=h: e.dma_start(
            out=vh[:], in_=kvf_dram[2048 + h * 128:2048 + h * 128 + 128, 0:1536]),
            inc=lh, inc_n=16, dma=True)
        ldh = ctx.now(lh)
        for pos in range(2):
            cs = slice(pos * 512, (pos + 1) * 512)
            ctx.op("sync", lambda e, h=h, pos=pos, cs=cs: e.dma_start(
                out=kd1[0:64, pos, :], in_=kvdiag[h * 128:h * 128 + 64, cs]),
                wait=(w_guard + [("kvd", kvd_done)] if pos == 0 and h == 0
                      else (w_guard if pos == 0 else ())),
                inc=lds, inc_n=16, dma=True)
            ctx.op("sync", lambda e, h=h, pos=pos, cs=cs: e.dma_start(
                out=kd2[0:64, pos, :],
                in_=kvdiag[h * 128 + 64:h * 128 + 128, cs]),
                inc=lds, inc_n=16, dma=True)
            ctx.op("sync", lambda e, h=h, pos=pos, cs=cs: e.dma_start(
                out=vd[:, pos, :],
                in_=kvdiag[2048 + h * 128:2048 + h * 128 + 128, cs]),
                inc=lds, inc_n=16, dma=True)
        ldd = ctx.now(lds)

        dv_vt = emit_transpose_tile(ctx, vh[:], vtok, 0, pt,
                                    wait_pe=[(lh, ldh)], src_chunks=12,
                                    extra_dv_wait=[("pe", last_head_pe)])

        for br, (qa, ka, kd) in enumerate([(q1a, k1a, kd1), (q2a, k2a, kd2)]):
            ctx.op("scalar", lambda e, ka=ka: e.activation(
                out=ksq[:, 0:1536], in_=ka[0:64, :], func=AF.Square),
                wait=[(lh, ldh), ("pe", prev_gn.get("ksq_pe", 0))])
            for pos in range(2):
                ctx.op("scalar", lambda e, kd=kd, pos=pos: e.activation(
                    out=ksq[:, 1536 + pos * 512:2048 + pos * 512],
                    in_=kd[0:64, pos, :], func=AF.Square),
                    wait=[(lds, ldd)])
            acq = ctx.now("ac")
            for blk in range(5):
                pe = ctx.op("tensor", lambda e, blk=blk: e.matmul(
                    ps_sum1[:], ctx.ones_col[0:64, :], ksq[:, blk * 512:(blk + 1) * 512],
                    start=True, stop=True),
                    wait=[("ac", acq), ("dv", prev_gn.get("sum1_dv", 0))])
                prev_gn["sum1_dv"] = ctx.op("vector", lambda e, blk=blk: e.tensor_copy(
                    out=norm_sc[:, blk * 512:(blk + 1) * 512], in_=ps_sum1[:]),
                    wait=[("pe", pe)])
            prev_gn["ksq_pe"] = pe
            ctx.op("vector", lambda e: e.reduce_max(
                out=c_sc[:, 0:1], in_=norm_sc[:], axis=AX))
            acc = ctx.op("scalar", lambda e: e.activation(
                out=c_sc[:, 0:1], in_=c_sc[:, 0:1], func=AF.Sqrt),
                wait=[("dv", ctx.now("dv"))])
            dvc = ctx.op("vector", lambda e: e.tensor_scalar_mul(
                out=c_sc[:, 1:2], in0=c_sc[:, 0:1], scalar1=-1.0),
                wait=[("ac", acc)])
            acq2 = ctx.op("scalar", lambda e, qa=qa: e.activation(
                out=ksq[:, 0:TPC], in_=qa[0:64, :], func=AF.Square))
            for blk in range(2):
                pe = ctx.op("tensor", lambda e, blk=blk: e.matmul(
                    ps_sum1[:], ctx.ones_col[0:64, :], ksq[:, blk * 512:(blk + 1) * 512],
                    start=True, stop=True),
                    wait=[("ac", acq2), ("dv", prev_gn.get("sum1_dv", 0))])
                prev_gn["sum1_dv"] = ctx.op("vector", lambda e, blk=blk: e.tensor_copy(
                    out=qn_sc[:, blk * 512:(blk + 1) * 512], in_=ps_sum1[:]),
                    wait=[("pe", pe)])
            prev_gn["ksq_pe"] = pe
            acq3 = ctx.op("scalar", lambda e: e.activation(
                out=qn_sc[:], in_=qn_sc[:], func=AF.Sqrt),
                wait=[("dv", prev_gn["sum1_dv"])])
            dvb = ctx.op("vector", lambda e, qa=qa: e.tensor_single_scalar(
                out=qa[64:65, :], in_=qn_sc[:], scalar=c_sc[:, 1:2], op=ALU.mult),
                wait=[("ac", acq3), ("dv", dvc)])
            prev_gn[f"brow{br}"] = dvb

        for pos, (qb, nfull) in enumerate(((1, 12), (0, 4))):
            slots = list(range(nfull)) + [12, 13, 14, 15]
            junk = set(range(8, 12)) if pos == 0 else set(range(0, 4))
            bankv = pt[pos % 2]
            wv = [(lds, ldd), ("pe", prev_gn.get("avpe1", 0)),
                  ("ld_c", ctx.consts_ld)] + bankv.guard()
            for j in range(4):
                pe = ctx.op("tensor", lambda e, j=j, pos=pos, bankv=bankv: e.transpose(
                    bankv[:, j * 128:(j + 1) * 128], vd[:, pos, j * 128:(j + 1) * 128],
                    ctx.ident[:]), wait=(wv if j == 0 else ()))
            dv_vd = ctx.op("vector", lambda e, bankv=bankv: e.tensor_copy(
                out=vtok[:, 12:16, :],
                in_=bankv[:, 0:512].rearrange("p (j c) -> p j c", c=128)),
                wait=[("pe", pe)])
            bankv.last_evac = ("dv", dv_vd)
            for br, (qa, kd, PT, ps_sum, ps_av) in enumerate(
                    [(q1a, kd1, PT1, ps_sum1, ps_av1),
                     (q2a, kd2, PT2, ps_sum2, ps_av2)]):
                ka = k1a if br == 0 else k2a
                for i, kt in enumerate(slots):
                    bank = ps_sc[i % 2]
                    w = bank.guard() + [("dv", dv_ones), ("dv", prev_gn[f"brow{br}"])]
                    if i == 0:
                        w.append(("po", prev_gn.get(f"pt_mask{br}", 0)))
                    if kt < 12:
                        lhs = ka[:, kt * 128:(kt + 1) * 128]
                    else:
                        lhs = kd[:, pos, (kt - 12) * 128:(kt - 11) * 128]
                    pe = ctx.op("tensor", lambda e, bank=bank, lhs=lhs, qa=qa, qb=qb:
                                e.matmul(bank[:], lhs, qa[:, qb * 512:(qb + 1) * 512],
                                         start=True, stop=True), wait=w)
                    if kt in junk:
                        ev = ctx.op("scalar", lambda e, bank=bank, PT=PT, kt=kt, pos=pos:
                                    e.activation(out=PT[:, kt, :], in_=bank[:],
                                                 func=AF.Exp,
                                                 bias=amask[:, pos:pos + 1]),
                                    wait=[("pe", pe), ("ld_lam", llam)])
                    else:
                        ev = ctx.op("scalar", lambda e, bank=bank, PT=PT, kt=kt:
                                    e.activation(out=PT[:, kt, :], in_=bank[:],
                                                 func=AF.Exp),
                                    wait=[("pe", pe)])
                    bank.last_evac = ("ac", ev)
                ac_exp = ctx.now("ac")
                for i in range(4):
                    prev_gn[f"pt_mask{br}"] = ctx.op("gpsimd",
                        lambda e, PT=PT, i=i: e.affine_select(
                            out=PT[:, 12 + i, :], in_=PT[:, 12 + i, :],
                            pattern=[[1, 512]],
                            compare_op=ALU.is_ge, fill=0.0,
                            base=-128 * i, channel_multiplier=-1),
                        wait=[("ac", ac_exp)])
                po_mask = ctx.now("po")
                for i, kt in enumerate(slots):
                    w = [("po", po_mask)] if i == 0 else ()
                    if i == 0:
                        w = list(w) + [("dv", prev_gn.get(f"sums_used{br}", 0))]
                    ctx.op("tensor", lambda e, PT=PT, kt=kt, i=i, ps_sum=ps_sum,
                           n=len(slots):
                           e.matmul(ps_sum[:], ctx.ones_col[:], PT[:, kt, :],
                                    start=(i == 0), stop=(i == n - 1)), wait=w)
                for i, kt in enumerate(slots):
                    w = []
                    if i == 0:
                        w = [("dv", prev_gn.get(f"av_used{br}", 0)), ("dv", dv_vd)]
                    pe_av = ctx.op("tensor", lambda e, PT=PT, kt=kt, i=i, ps_av=ps_av,
                                   n=len(slots):
                                   e.matmul(ps_av[:], vtok[:, kt, :], PT[:, kt, :],
                                            start=(i == 0), stop=(i == n - 1)),
                                   wait=w)
                prev_gn[f"avpe{br}"] = pe_av
            last_head_pe = ctx.now("pe")
            dvr1 = ctx.op("vector", lambda e: e.reciprocal(
                out=rs_sc[:, 0:512], in_=ps_sum1[:]),
                wait=[("pe", ctx.now("pe")), ("ac", ctx.now("ac")), ("po", ctx.now("po"))])
            dvr2 = ctx.op("vector", lambda e: e.reciprocal(
                out=rs_sc[:, 512:1024], in_=ps_sum2[:]))
            prev_gn["sums_used0"] = prev_gn["sums_used1"] = dvr2
            for br, dvr in ((0, dvr1), (1, dvr2)):
                pe = ctx.op("tensor", lambda e, br=br: e.matmul(
                    ps_sc[0][:], onesrow_f[:], rs_sc[:, br * 512:(br + 1) * 512],
                    start=True, stop=True),
                    wait=[("dv", dvr), ("dv", prev_gn.get("bc_used", 0))] + ps_sc[0].guard())
                ps_sc[0].last_evac = ("ac", ctx.op("scalar", lambda e, br=br, qb=qb: e.copy(
                    out=bc[:, br, qb * 512:(qb + 1) * 512], in_=ps_sc[0][:]),
                    wait=[("pe", pe)]))
            ac_bc = ctx.now("ac")
            dv_t1 = ctx.op("vector", lambda e, qb=qb: e.tensor_tensor(
                out=t1s[:], in0=ps_av1[:], in1=bc[:, 0, qb * 512:(qb + 1) * 512],
                op=ALU.mult), wait=[("ac", ac_bc)])
            dv_t2 = ctx.op("vector", lambda e, qb=qb: e.tensor_tensor(
                out=t2s[:], in0=ps_av2[:], in1=bc[:, 1, qb * 512:(qb + 1) * 512],
                op=ALU.mult))
            prev_gn["av_used0"] = prev_gn["av_used1"] = dv_t2
            dv_ac = ctx.op("vector", lambda e, qb=qb: e.scalar_tensor_tensor(
                out=Acomb[:, qb * 512:(qb + 1) * 512], in0=t2s[:], scalar=lamt[:, 5:6],
                in1=t1s[:], op0=ALU.mult, op1=ALU.add),
                wait=[("dv", dv_lam)])
            ac_sq = ctx.op("scalar", lambda e, qb=qb: e.activation(
                out=Asq[:, qb * 512:(qb + 1) * 512], in_=Acomb[:, qb * 512:(qb + 1) * 512],
                func=AF.Square), wait=[("dv", dv_ac)])
            pe_s1 = ctx.op("tensor", lambda e, qb=qb: e.matmul(
                ps_sum1[:], onescol_f[:], Acomb[:, qb * 512:(qb + 1) * 512],
                start=True, stop=True), wait=[("dv", dv_ac), ("dv", dvr2)])
            pe_s2 = ctx.op("tensor", lambda e, qb=qb: e.matmul(
                ps_sum2[:], onescol_f[:], Asq[:, qb * 512:(qb + 1) * 512],
                start=True, stop=True), wait=[("ac", ac_sq)])
            dv_m = ctx.op("vector", lambda e: e.tensor_scalar_mul(
                out=mrow[:], in0=ps_sum1[:], scalar1=1.0 / 128), wait=[("pe", pe_s2)])
            ctx.op("vector", lambda e: e.tensor_tensor(
                out=vrow[:], in0=mrow[:], in1=mrow[:], op=ALU.mult))
            dv_v = ctx.op("vector", lambda e: e.scalar_tensor_tensor(
                out=vrow[:], in0=ps_sum2[:], scalar=1.0 / 128, in1=vrow[:],
                op0=ALU.mult, op1=ALU.subtract))
            prev_gn["sums_used0"] = prev_gn["sums_used1"] = dv_v
            prev_gn["sum1_dv"] = dv_v
            ac_sd = ctx.op("scalar", lambda e: e.activation(
                out=vrow[:], in_=vrow[:], func=AF.Sqrt, bias=cc(ctx, C_GNEPS)[0:1, :]),
                wait=[("dv", dv_v)])
            dv_rsd = ctx.op("vector", lambda e: e.reciprocal(out=vrow[:], in_=vrow[:]),
                            wait=[("ac", ac_sd)])
            for slot, row, dvw in ((2, mrow, dv_m), (3, vrow, dv_rsd)):
                pe = ctx.op("tensor", lambda e, row=row: e.matmul(
                    ps_sc[0][:], onesrow_f[:], row[:], start=True, stop=True),
                    wait=[("dv", dvw)] + ps_sc[0].guard())
                ps_sc[0].last_evac = ("ac", ctx.op("scalar", lambda e, slot=slot, qb=qb: e.copy(
                    out=bc[:, slot, qb * 512:(qb + 1) * 512], in_=ps_sc[0][:]),
                    wait=[("pe", pe)]))
            ac_bc2 = ctx.now("ac")
            dvu = ctx.op("vector", lambda e, qb=qb: e.tensor_tensor(
                out=t1s[:], in0=Acomb[:, qb * 512:(qb + 1) * 512],
                in1=bc[:, 2, qb * 512:(qb + 1) * 512], op=ALU.subtract),
                wait=[("ac", ac_bc2)])
            dvu = ctx.op("vector", lambda e, qb=qb: e.tensor_tensor(
                out=t1s[:], in0=t1s[:], in1=bc[:, 3, qb * 512:(qb + 1) * 512],
                op=ALU.mult))
            dvw = ctx.op("vector", lambda e, h=h, qb=qb: e.tensor_scalar(
                out=WoIn[:, h, qb * 512:(qb + 1) * 512], in0=t1s[:],
                scalar1=gnt[:, h:h + 1], scalar2=gnt[:, 16 + h:17 + h],
                op0=ALU.mult, op1=ALU.add), wait=[("ac", ac_gn)])
            prev_gn["bc_used"] = dvw
    dv_woin = ctx.now("dv")

    ctx.pop_scope()
    reuse_bar = ctx.barrier_wait()
    wob = [ctx.sb(f"wob{i}", [128, DC, 512], BF16) for i in range(2)]
    osb = ctx.sb("a0osb", [128, 16, TPC], BF16)
    wsrc = wot_in.rearrange("(c p) f -> p c f", p=128)
    pw = [ps_av1, ps_av2]
    last_wo_pe = {}
    for ob in range(4):
        wb = wob[ob % 2]
        wld = ctx.op("gpsimd", lambda e, ob=ob, wb=wb: e.dma_start(
            out=wb[:], in_=wsrc[:, :, ob * 512:(ob + 1) * 512]),
            wait=list(reuse_bar) + [("pe", last_wo_pe.get(ob % 2, 0))],
            inc=f"wl{ob % 2}", inc_n=16, dma=True)
        wldc = ctx.now(f"wl{ob % 2}")
        for fc in range(4):
            for qbh in range(2):
                bank = pw[(fc * 2 + qbh) % 2]
                w = list(bank.guard())
                if fc == 0 and qbh == 0:
                    w += [(f"wl{ob % 2}", wldc), ("dv", dv_woin),
                          ("pe", prev_gn.get("avpe1", 0))]
                for d in range(DC):
                    pe = ctx.op("tensor", lambda e, wb=wb, d=d, fc=fc, qbh=qbh, bank=bank:
                                e.matmul(bank[:], wb[:, d, fc * 128:(fc + 1) * 128],
                                         WoIn[:, d, qbh * 512:(qbh + 1) * 512],
                                         start=(d == 0), stop=(d == DC - 1)),
                                wait=w if d == 0 else ())
                bank.last_evac = ("ac", ctx.op("scalar",
                    lambda e, bank=bank, ob=ob, fc=fc, qbh=qbh: e.copy(
                        out=osb[:, ob * 4 + fc, qbh * 512:(qbh + 1) * 512], in_=bank[:]),
                    wait=[("pe", pe)]))
        last_wo_pe[ob % 2] = ctx.now("pe")
    ac_osb = ctx.now("ac")

    for t in range(NT):
        for g in range(2):
            bank = pt[g % 2]
            w = [("ac", ac_osb), ("ld_x", ld_x)] + bank.guard()
            for j in range(8):
                d = g * 8 + j
                pe = ctx.op("tensor", lambda e, d=d, j=j, t=t, bank=bank: e.transpose(
                    bank[:, j * 128:(j + 1) * 128], osb[:, d, t * 128:(t + 1) * 128],
                    ctx.ident[:]), wait=w if j == 0 else ())
            bank.last_evac = ("dv", ctx.op("vector", lambda e, t=t, g=g, bank=bank: e.tensor_tensor(
                out=X[:, t, g * 1024:(g + 1) * 1024], in0=X[:, t, g * 1024:(g + 1) * 1024],
                in1=bank[:], op=ALU.add), wait=[("pe", pe)]))

    ctx.pop_scope()
    return ctx.now("dv")


def emit_ffn0_part(ctx, nc, io, bar_a0):
    wgt_in = io["wgt_in"]
    wut_in = io["wut_in"]
    wdt_in = io["wdt_in"]
    pT_dram = io["pT_dram"]
    X = io["X"]
    stat_m = io["stat_m"]
    stat_r = io["stat_r"]
    H = 8192
    ctx.push_scope()
    rs_arr = ctx.sb("rs_arr", [128, NT], F32)
    s_arr = ctx.sb("s_arr", [128, NT], F32)
    HT = ctx.sb("f0HT", [128, DC, TPC], BF16)
    ctx.push_scope()
    sq_scr = ctx.sb("f0sq_scr", [128, D], F32)
    xq = ctx.sb("xq", [128, D], F32)
    qa = ctx.sb("qa", [128, D], F32)
    ian = ctx.sb("ian", [128, D], I32)
    fan = ctx.sb("fan", [128, D], F32)
    rrb = ctx.sb("rrb", [128, D], F32)
    sg = ctx.sb("sg", [128, D], F32)
    kqb = ctx.sb("kqb", [128, D], BF16)
    rho = ctx.sb("f0rho", [128, 1], F32)
    pt = [Bank(ctx.ps("f0pt0", [128, 1024], BF16)), Bank(ctx.ps("f0pt1", [128, 1024], BF16))]
    ctx.wait("vector", [("ld_c", ctx.consts_ld)])
    for t in range(NT):
        emit_ln_stats(ctx, X, t, stat_m, stat_r, sq_scr,
                      wait_v=list(bar_a0), wait_s=list(bar_a0))
        m = stat_m[:, t:t + 1]
        r = stat_r[:, t:t + 1]
        ctx.op("vector", lambda e, r=r: e.tensor_tensor(out=rho[:], in0=r, in1=r, op=ALU.mult))
        dvp = ctx.op("vector", lambda e: e.tensor_scalar(
            out=rho[:], in0=rho[:], scalar1=-1e-5, scalar2=1.0 + 1e-8,
            op0=ALU.mult, op1=ALU.add))
        acs = ctx.op("scalar", lambda e: e.activation(out=rho[:], in_=rho[:], func=AF.Sqrt),
                     wait=[("dv", dvp)])
        ctx.op("vector", lambda e: e.reciprocal(out=rho[:], in_=rho[:]), wait=[("ac", acs)])
        ctx.op("vector", lambda e, r=r: e.tensor_tensor(out=rho[:], in0=rho[:], in1=r, op=ALU.mult))
        dvx = ctx.op("vector", lambda e, t=t, m=m: e.tensor_scalar(
            out=xq[:], in0=X[:, t, :], scalar1=m, scalar2=rho[:],
            op0=ALU.subtract, op1=ALU.mult))
        dvm = ctx.op("vector", lambda e, t=t: e.reduce_max(
            out=s_arr[:, t:t + 1], in_=xq[:], axis=AX, apply_absolute_value=True))
        ctx.op("vector", lambda e, t=t: e.tensor_scalar_mul(
            out=rs_arr[:, t:t + 1], in0=s_arr[:, t:t + 1], scalar1=1.0 / 127.0))
        ctx.op("vector", lambda e, t=t: e.reciprocal(
            out=s_arr[:, t:t + 1], in_=s_arr[:, t:t + 1]))
        dvs = ctx.op("vector", lambda e, t=t: e.tensor_scalar_mul(
            out=s_arr[:, t:t + 1], in0=s_arr[:, t:t + 1], scalar1=127.0))
        ac_sg = ctx.op("scalar", lambda e: e.sign(out=sg[:], in_=xq[:]), wait=[("dv", dvx)])
        ac_ab = ctx.op("scalar", lambda e: e.activation(out=qa[:], in_=xq[:], func=AF.Abs))
        dvu = ctx.op("vector", lambda e, t=t: e.tensor_scalar(
            out=qa[:], in0=qa[:], scalar1=s_arr[:, t:t + 1], scalar2=0.5,
            op0=ALU.mult, op1=ALU.add), wait=[("ac", ac_ab), ("dv", dvs)])
        emit_floor(ctx, qa[:], ian[:], fan[:], rrb[:], xq[:])
        dvk = ctx.op("vector", lambda e: e.scalar_tensor_tensor(
            out=kqb[:], in0=xq[:], scalar=127.0, in1=sg[:], op0=ALU.min, op1=ALU.mult),
            wait=[("ac", ac_sg)])
        emit_transpose_tile(ctx, kqb[:], HT, t, pt, wait_pe=[("dv", dvk)])
    dv_kqt = ctx.now("dv")
    ctx.pop_scope()
    barA = ctx.barrier_wait()

    ctx.push_scope()
    wgb = ctx.sb("wgb", [128, DC, 512], BF16)
    wub = ctx.sb("wub", [128, DC, 512], BF16)
    gsb = ctx.sb("gsb", [128, 512], F32)
    usb = ctx.sb("usb", [128, 512], F32)
    pb = ctx.sb("pb", [128, 512], BF16)
    ptst = [ctx.sb(f"ptst{i}", [128, 512], BF16) for i in range(2)]
    psg = Bank(ctx.ps("psg", [128, 512], F32))
    psu = Bank(ctx.ps("psu", [128, 512], F32))
    ptB = [Bank(ctx.ps("ptB0", [128, 1024], BF16)), Bank(ctx.ps("ptB1", [128, 1024], BF16))]
    wg_src = wgt_in.rearrange("(c p) f -> p c f", p=128)
    wu_src = wut_in.rearrange("(c p) f -> p c f", p=128)
    last_mm = 0
    st_cnt = {}
    for ob in range(16):
        lw = f"wl{ob % 2}"
        w0 = list(barA) if ob == 0 else [("pe", last_mm)]
        ctx.op("gpsimd", lambda e, ob=ob: e.dma_start(
            out=wgb[:], in_=wg_src[:, :, ob * 512:(ob + 1) * 512]),
            wait=w0, inc=lw, inc_n=16, dma=True)
        ctx.op("gpsimd", lambda e, ob=ob: e.dma_start(
            out=wub[:], in_=wu_src[:, :, ob * 512:(ob + 1) * 512]),
            inc=lw, inc_n=16, dma=True)
        lwc = ctx.now(lw)
        acg = ctx.op("scalar", lambda e: e.sign(
            out=wgb[:].rearrange("p c f -> p (c f)"), in_=wgb[:].rearrange("p c f -> p (c f)")),
            wait=[(lw, lwc)])
        acu = ctx.op("scalar", lambda e: e.sign(
            out=wub[:].rearrange("p c f -> p (c f)"), in_=wub[:].rearrange("p c f -> p (c f)")))
        for t in range(NT):
            wmm = [("ac", acu), ("dv", dv_kqt)] + psg.guard() + psu.guard()
            for d in range(DC):
                ctx.op("tensor", lambda e, d=d, t=t: e.matmul(
                    psg[:], HT[:, d, t * 128:(t + 1) * 128], wgb[:, d, :],
                    start=(d == 0), stop=(d == DC - 1)),
                    wait=wmm if d == 0 else ())
                pe_mm = ctx.op("tensor", lambda e, d=d, t=t: e.matmul(
                    psu[:], HT[:, d, t * 128:(t + 1) * 128], wub[:, d, :],
                    start=(d == 0), stop=(d == DC - 1)))
            psg.last_evac = ("ac", ctx.op("scalar", lambda e, t=t: e.activation(
                out=gsb[:], in_=psg[:], func=AF.Silu, scale=rs_arr[:, t:t + 1]),
                wait=[("pe", pe_mm)]))
            psu.last_evac = ("dv", ctx.op("vector", lambda e, t=t: e.tensor_single_scalar(
                out=usb[:], in_=psu[:], scalar=rs_arr[:, t:t + 1], op=ALU.mult),
                wait=[("pe", pe_mm)]))
            dv_p = ctx.op("vector", lambda e: e.tensor_tensor(
                out=pb[:], in0=gsb[:], in1=usb[:], op=ALU.mult),
                wait=[psg.last_evac])
            bankB = ptB[t % 2]
            stg = ptst[t % 2]
            wtr = [("dv", dv_p)] + bankB.guard()
            for j in range(4):
                pe_tr = ctx.op("tensor", lambda e, j=j, bankB=bankB: e.transpose(
                    bankB[:, j * 128:(j + 1) * 128], pb[:, j * 128:(j + 1) * 128],
                    ctx.ident[:]), wait=wtr if j == 0 else ())
            bankB.last_evac = ("dv", ctx.op("vector", lambda e, bankB=bankB, stg=stg: e.tensor_copy(
                out=stg[:], in_=bankB[:, 0:512]),
                wait=[("pe", pe_tr), (f"stp{t % 2}", st_cnt.get(t % 2, 0))]))
            for j in range(4):
                st_cnt[t % 2] = ctx.op("sync", lambda e, ob=ob, t=t, j=j, stg=stg: e.dma_start(
                    out=pT_dram[ob * 512 + j * 128:ob * 512 + (j + 1) * 128,
                                t * 128:(t + 1) * 128], in_=stg[:, j * 128:(j + 1) * 128]),
                    wait=[bankB.last_evac] if j == 0 else (),
                    inc=f"stp{t % 2}", inc_n=16, dma=True)
        last_mm = ctx.now("pe")
    pT_done = [("stp0", ctx.now("stp0")), ("stp1", ctx.now("stp1"))]
    ctx.pop_scope()
    barB = ctx.barrier_wait()

    ctx.push_scope()
    ptc = [ctx.sb(f"ptc{i}", [128, TPC], BF16) for i in range(2)]
    wdc = [ctx.sb(f"wdc{i}", [128, 512], BF16) for i in range(2)]
    psd = [Bank(ctx.ps(f"psd{i}", [128, 512], F32)) for i in range(NT)]
    wd_src = wdt_in.rearrange("(c p) f -> p c f", p=128)
    NF = H // 128
    ctx._ptc_use = {}
    for oq in range(4):
        for f in range(NF):
            lp = f"lp{f % 2}"
            lw = f"lw{f % 2}"
            first = (oq == 0 and f < 2)
            ctx.op("gpsimd", lambda e, f=f, oq=oq: e.dma_start(
                out=ptc[f % 2][:], in_=pT_dram[f * 128:(f + 1) * 128, :]),
                wait=(list(barB) + list(pT_done) if first else []) +
                     [("pe", ctx._ptc_use.get(f % 2, 0))],
                inc=lp, inc_n=16, dma=True)
            ctx.op("gpsimd", lambda e, f=f, oq=oq: e.dma_start(
                out=wdc[f % 2][:], in_=wd_src[:, f, oq * 512:(oq + 1) * 512]),
                inc=lw, inc_n=16, dma=True)
            lwc, lpc = ctx.now(lw), ctx.now(lp)
            ac_sd = ctx.op("scalar", lambda e, f=f: e.sign(
                out=wdc[f % 2][:], in_=wdc[f % 2][:]), wait=[(lw, lwc)])
            for t in range(NT):
                wmm = []
                if t == 0:
                    wmm = [(lp, lpc), ("ac", ac_sd)]
                    if f == 0:
                        wmm += psd[0].guard() + psd[NT - 1].guard()
                pe_d = ctx.op("tensor", lambda e, f=f, t=t, oq=oq: e.matmul(
                    psd[t][:], ptc[f % 2][:, t * 128:(t + 1) * 128], wdc[f % 2][:],
                    start=(f == 0), stop=(f == NF - 1)),
                    wait=wmm)
            ctx._ptc_use[f % 2] = ctx.now("pe")
        for t in range(NT):
            psd[t].last_evac = ("dv", ctx.op("vector", lambda e, t=t, oq=oq: e.tensor_tensor(
                out=X[:, t, oq * 512:(oq + 1) * 512], in0=X[:, t, oq * 512:(oq + 1) * 512],
                in1=psd[t][:], op=ALU.add), wait=[("pe", ctx.now("pe"))] if t == 0 else ()))

    ctx.pop_scope()
    return ctx.barrier_wait()


def emit_qkv1_part(ctx, nc, io, bar_f0):
    wcatT_in = io["wcatT_in"]
    ns_in = io["ns_in"]
    ss_in = io["ss_in"]
    q1f_out = io["q1f_out"]
    kv1_out = io["kv1_out"]
    x2_out = io["x2_out"]
    X = io["X"]
    stat_m = io["stat_m"]
    stat_r = io["stat_r"]
    ctx.push_scope()
    sq_scr = ctx.sb("q1sq_scr", [128, D], F32)
    HT = ctx.sb("q1HT", [128, DC, TPC], BF16)
    h_bf = [ctx.sb(f"q1h_bf{i}", [128, D], BF16) for i in range(2)]
    h_f32 = ctx.sb("q1h_f32", [128, D], F32)
    rho_scr = ctx.sb("q1rho", [128, 1], F32)
    csb = ctx.sb("q1csb", [128, D], F32)
    csrow = ctx.sb("q1csrow", [1, D], F32)
    onesrow_f = ctx.sb("q1ones", [1, 128], F32)
    pt = [Bank(ctx.ps(f"q1pt{i}", [128, 1024], BF16)) for i in range(2)]
    wblk = [ctx.sb(f"q1wblk{i}", [128, DC, 384], BF16) for i in range(2)]
    stg = [ctx.sb(f"q1stg{i}", [128, TPC], BF16) for i in range(2)]
    pq = [Bank(ctx.ps(f"q1pq{i}", [128, 512], F32)) for i in range(4)]
    pbc = Bank(ctx.ps("q1pbc", [128, 512], F32))
    for t in range(NT):
        ctx.op("sync", lambda e, t=t: e.dma_start(
            out=x2_out.rearrange("(t p) d -> p t d", p=128)[:, t, :], in_=X[:, t, :]),
            wait=list(bar_f0) if t == 0 else [], inc="st_x2", inc_n=16, dma=True)
    ctx.op("sync", lambda e: e.dma_start(out=csrow[:], in_=ns_in),
           wait=list(bar_f0), inc="ld_cs", inc_n=16, dma=True)
    ctx.op("sync", lambda e: e.dma_start(out=sq_scr[0:1, :], in_=ss_in),
           inc="ld_cs", inc_n=16, dma=True)
    lw = ctx.now("ld_cs")
    ctx.op("vector", lambda e: e.tensor_tensor(out=csrow[:], in0=csrow[:],
                                               in1=sq_scr[0:1, :], op=ALU.mult),
           wait=[("ld_cs", lw)] + list(bar_f0))
    dvc = ctx.op("vector", lambda e: e.memset(onesrow_f[:], 1.0))
    prev_cp = 0
    for j in range(4):
        pe = ctx.op("tensor", lambda e, j=j: e.matmul(
            pbc[:], onesrow_f[:], csrow[:, j * 512:(j + 1) * 512],
            start=True, stop=True), wait=[("dv", dvc), ("ac", prev_cp)] + list(bar_f0))
        prev_cp = ctx.op("scalar", lambda e, j=j: e.copy(
            out=csb[:, j * 512:(j + 1) * 512], in_=pbc[:]), wait=[("pe", pe)])
    cs_ac = ctx.now("ac")
    emit_ln_h_transpose(ctx, X, NT, stat_m, stat_r, sq_scr, h_bf, HT, pt,
                        list(bar_f0), l1cs=(csb, rho_scr, h_f32, cs_ac))
    wsrc1 = wcatT_in.rearrange("(c p) f -> p c f", p=128)
    st_track = {}

    def qkv1_sink(ctx, f_abs, bks, pe):
        sg = stg[(f_abs // 128) % 2]
        key = ("s", (f_abs // 128) % 2)
        w = [("pe", pe)]
        if st_track.get(key):
            w.append(st_track[key])
        scale = 0.125 if f_abs < 2048 else 1.0
        a1 = ctx.op("scalar", lambda e, sg=sg, b=bks[0], scale=scale: e.activation(
            out=sg[:, 0:512], in_=b[:], func=AF.Copy, scale=scale), wait=w)
        a2 = ctx.op("scalar", lambda e, sg=sg, b=bks[1], scale=scale: e.activation(
            out=sg[:, 512:1024], in_=b[:], func=AF.Copy, scale=scale))
        bks[0].last_evac = ("ac", a1)
        bks[1].last_evac = ("ac", a2)
        ssem = f"stq1{(f_abs // 128) % 2}"
        if f_abs < 2048:
            stc = ctx.op("sync", lambda e, sg=sg, f_abs=f_abs: e.dma_start(
                out=q1f_out[f_abs:f_abs + 128, :], in_=sg[:]),
                wait=[("ac", a2)], inc=ssem, inc_n=16, dma=True)
        else:
            stc = ctx.op("sync", lambda e, sg=sg, f_abs=f_abs: e.dma_start(
                out=kv1_out[f_abs - 2048:f_abs - 2048 + 128, :], in_=sg[:]),
                wait=[("ac", a2)], inc=ssem, inc_n=16, dma=True)
        st_track[key] = (ssem, stc)

    emit_proj_stream(ctx, HT, NT, wsrc1, 0, 2304, wblk, pq, qkv1_sink,
                     [("dv", ctx.now("dv"))], "w1q", FB=384)
    done = [(f"stq1{i}", ctx.now(f"stq1{i}")) for i in range(2)] +            [("st_x2", ctx.now("st_x2"))]
    ctx.pop_scope()
    return done


def emit_wprep_part(ctx, nc, io, start_bar):
    I = 5632
    wsc_dram = io["wsc_dram"]
    wgT = io["wgT_sh"]
    wuT = io["wuT_sh"]
    wdT = io["wdT_sh"]
    qg = io["qwgT_out"]
    qu = io["qwuT_out"]
    qd = io["qwdT_out"]
    ctx.push_scope()
    ctx.push_scope()
    nat = ctx.sb("wpnat", [128, D], F32)
    red = ctx.sb("wpred", [128, 3], F32)
    for (wn, base, rows, cols) in ((io["wg_nat"], 0, I, D),
                                   (io["wu_nat"], I, I, D),
                                   (io["wd_nat"], 2 * I, D, I)):
        nt_ = (rows + 127) // 128
        for ti in range(nt_):
            p = min(128, rows - ti * 128)
            for cb in range(0, cols, D):
                cw = min(D, cols - cb)
                lsem = f"wpl{ti % 2}"
                lc = ctx.op("sync", lambda e, wn=wn, ti=ti, p=p, cb=cb, cw=cw: e.dma_start(
                    out=nat[0:p, 0:cw], in_=wn[ti * 128:ti * 128 + p, cb:cb + cw]),
                    wait=list(start_bar) + [("dv", ctx.now("dv"))],
                    inc=lsem, inc_n=16, dma=True)
                lcc = ctx.now(lsem)
                ctx.op("vector", lambda e, p=p, cw=cw: e.reduce_max(
                    out=red[0:p, 0:1], in_=nat[0:p, 0:cw], axis=AX,
                    apply_absolute_value=True),
                    wait=[(lsem, lcc), ("st_wsc", ctx.now("st_wsc"))])
                if cb == 0:
                    dvp = ctx.op("vector", lambda e, p=p: e.tensor_copy(
                        out=red[0:p, 2:3], in_=red[0:p, 0:1]))
                else:
                    dvp = ctx.op("vector", lambda e, p=p: e.tensor_tensor(
                        out=red[0:p, 2:3], in0=red[0:p, 2:3], in1=red[0:p, 0:1],
                        op=ALU.max))
            dvp = ctx.op("vector", lambda e, p=p: e.tensor_scalar_mul(
                out=red[0:p, 2:3], in0=red[0:p, 2:3], scalar1=1.0 / 448.0))
            ctx.op("sync", lambda e, base=base, ti=ti, p=p: e.dma_start(
                out=wsc_dram[base + ti * 128:base + ti * 128 + p, None],
                in_=red[0:p, 2:3]),
                wait=[("dv", dvp)], inc="st_wsc", inc_n=16, dma=True)
    ctx.pop_scope()
    sc_ready = [("st_wsc", ctx.now("st_wsc"))]
    scg = wsc_dram[0:I][None, :]
    scu = wsc_dram[I:2 * I][None, :]
    scd = wsc_dram[2 * I:2 * I + D][None, :]
    specs = ((wgT, scg, qg, 256, 5632), (wuT, scu, qu, 256, 5632),
             (wdT, scd, qd, 704, 2048))
    onesrow_f = ctx.sb("wponesrow_f", [1, 128], F32)
    ctx.op("vector", lambda e: e.memset(onesrow_f[:], 1.0))
    scrow = ctx.sb("wpscrow", [1, 5632], F32)
    bc_s = ctx.sb("bc_s", [128, 5632], F32)
    bc_w = ctx.sb("bc_w", [128, 5632], BF16)
    tile = ctx.sb("wptile", [128, 2048], F32)
    outb = ctx.sb("wpoutb", [128, 5632], BF16)
    QB = 1024
    qs_t = {k: ctx.sb(f"wpqs_{k}", [128, QB], F32 if k != "ian" else I32)
            for k in ("a", "ian", "fan", "rr", "ef", "ui", "t1")}
    ps_b = Bank(ctx.ps("wpps_b", [128, 512], F32))

    for (wsrc, scsrc, qdst, rows, F) in specs:
        lsc = ctx.op("sync", lambda e, scsrc=scsrc, F=F: e.dma_start(
            out=scrow[:, 0:F], in_=scsrc),
            wait=list(sc_ready) + [("ac", ctx.now("ac"))], inc="lsc", inc_n=16, dma=True)
        emit_bcast_row(ctx, scrow[:, 0:F], bc_w, F, ps_b, onesrow_f,
                       wait=[("lsc", lsc), ("ld_c", ctx.consts_ld),
                             ("dv", ctx.now("dv"))])
        dvv = ctx.op("vector", lambda e, F=F: e.reciprocal(
            out=scrow[:, 0:F], in_=scrow[:, 0:F]),
            wait=[("ac", ctx.now("ac"))])
        ac_b = emit_bcast_row(ctx, scrow[:, 0:F], bc_s, F, ps_b, onesrow_f,
                              wait=[("dv", dvv)])
        r0 = 0
        while r0 < rows:
            p = min(128, rows - r0)
            for j in range(0, F, QB):
                wj = min(QB, F - j)
                lw = f"wqlw{(j // QB) % 2}"
                lc = ctx.op("sync", lambda e, wsrc=wsrc, r0=r0, p=p, j=j, wj=wj:
                            e.dma_start(out=tile[0:p, 0:wj],
                                        in_=wsrc[r0:r0 + p, j:j + wj]),
                            wait=[("st_o", ctx.now("st_o")), ("dv", ctx.now("dv"))],
                            inc=lw, inc_n=16, dma=True)
                dvw = ctx.op("vector", lambda e, j=j, wj=wj, p=p: e.tensor_tensor(
                    out=tile[0:p, 0:wj],
                    in0=tile[0:p, 0:wj], in1=bc_s[0:p, j:j + wj], op=ALU.mult),
                    wait=[(lw, lc), ("ac", ac_b)])
                qv = {k: qs_t[k][0:p, 0:wj] for k in qs_t}
                emit_fp8_quant(ctx, tile[0:p, 0:wj], outb[0:p, j:j + wj], qv,
                               wait=[("dv", dvw)])
                dvq = ctx.op("vector", lambda e, j=j, wj=wj, p=p: e.tensor_tensor(
                    out=outb[0:p, j:j + wj], in0=outb[0:p, j:j + wj],
                    in1=bc_w[0:p, j:j + wj], op=ALU.mult))
            ctx.op("sync", lambda e, qdst=qdst, r0=r0, p=p, F=F: e.dma_start(
                out=qdst[r0:r0 + p, :], in_=outb[0:p, 0:F]),
                wait=[("dv", ctx.now("dv"))], inc="st_o", inc_n=16, dma=True)
            r0 += p

    ctx.pop_scope()
    return [("st_o", ctx.now("st_o"))]


def build_launch1_full():
    ctx, nc, io = build_launch1()
    qf, kvf, qd_, kvd_ = ctx._l1_sources
    emit_attn0_part(ctx, nc, io, qf, kvf, list(qd_), list(kvd_))
    bar_a0 = ctx.barrier_wait()
    bar_f0 = emit_ffn0_part(ctx, nc, io, bar_a0)
    done_q1 = emit_qkv1_part(ctx, nc, io, bar_f0)
    done_wp = emit_wprep_part(ctx, nc, io, ctx.barrier_wait() + list(done_q1))
    ctx.wait("sync", done_q1 + done_wp)
    ctx.build_and_close()
    return nc



def kernel(L0_attn_x, L0_attn_W_qkv, L0_attn_W_o, L0_attn_lambda_q1,
           L0_attn_lambda_k1, L0_attn_lambda_q2, L0_attn_lambda_k2,
           L0_attn_gn_gamma, L0_attn_gn_beta, L0_ffn_W_g, L0_ffn_W_u,
           L0_ffn_W_d, L1_attn_norm_scale, L1_attn_split_scale, L1_attn_Wq,
           L1_attn_Wk, L1_attn_Wv, L1_attn_Wo, L1_ffn_w_gate, L1_ffn_w_up,
           L1_ffn_w_down, L1_ffn_up_reorder_index, L1_ffn_down_reorder_index):
    _ = L1_ffn_up_reorder_index
    f32 = np.float32
    x = np.ascontiguousarray(np.asarray(L0_attn_x, f32))
    xs = [_carve(x, c) for c in range(8)]
    wqkvT = np.ascontiguousarray(np.asarray(L0_attn_W_qkv, f32).T)
    lam4 = np.ascontiguousarray(np.stack([
        np.asarray(L0_attn_lambda_q1, f32), np.asarray(L0_attn_lambda_k1, f32),
        np.asarray(L0_attn_lambda_q2, f32), np.asarray(L0_attn_lambda_k2, f32)]))
    gnt = np.ascontiguousarray(np.concatenate(
        [np.asarray(L0_attn_gn_gamma, f32).T, np.asarray(L0_attn_gn_beta, f32).T],
        axis=1))
    wot0 = np.ascontiguousarray(np.asarray(L0_attn_W_o, f32).T)
    wgt = np.ascontiguousarray(np.asarray(L0_ffn_W_g, f32).T)
    wut = np.ascontiguousarray(np.asarray(L0_ffn_W_u, f32).T)
    wdt = np.ascontiguousarray(np.asarray(L0_ffn_W_d, f32).T)
    Wq = np.asarray(L1_attn_Wq, f32)
    Wk = np.asarray(L1_attn_Wk, f32)
    Wv = np.asarray(L1_attn_Wv, f32)
    wcatT = np.ascontiguousarray(np.concatenate([Wq, Wk, Wv], axis=0).T)
    ns = np.ascontiguousarray(np.asarray(L1_attn_norm_scale, f32)[None, :])
    ss = np.ascontiguousarray(np.asarray(L1_attn_split_scale, f32)[None, :])
    wg_n = np.ascontiguousarray(np.asarray(L1_ffn_w_gate, f32))
    wu_n = np.ascontiguousarray(np.asarray(L1_ffn_w_up, f32))
    wd_n = np.ascontiguousarray(np.asarray(L1_ffn_w_down, f32))
    wgT_n = np.ascontiguousarray(wg_n.T)
    wuT_n = np.ascontiguousarray(wu_n.T)
    wdT_n = np.ascontiguousarray(wd_n.T)

    def _amask(c):
        a = np.zeros((128, 2), np.float32)
        a[:, 1 - c % 2] = -30000.0
        return a

    nc = _get_nc("launch1", build_launch1_full)
    rr = run_bass_kernel_spmd(nc, [
        {"x_in": xs[c], "xf_in": np.ascontiguousarray(x[c // 2]),
         "amask_in": _amask(c),
         "wqkvT_in": wqkvT, "lam_in": lam4, "gnt_in": gnt, "wot_in": wot0,
         "wgt_in": wgt, "wut_in": wut, "wdt_in": wdt,
         "wcatT_in": wcatT, "ns_in": ns, "ss_in": ss,
         "wg_nat": wg_n, "wu_nat": wu_n, "wd_nat": wd_n,
         "wgT_sh": np.ascontiguousarray(wgT_n[256 * c:256 * (c + 1)]),
         "wuT_sh": np.ascontiguousarray(wuT_n[256 * c:256 * (c + 1)]),
         "wdT_sh": np.ascontiguousarray(wdT_n[704 * c:704 * (c + 1)])}
        for c in range(8)], core_ids=list(range(8)))
    res1 = rr.results

    x2 = [np.asarray(res1[c]["x2_out"]) for c in range(8)]
    kv1_seq = _seq_order_T([np.asarray(res1[c]["kv1_out"]) for c in range(8)])
    qwgT = np.ascontiguousarray(np.concatenate(
        [np.asarray(res1[c]["qwgT_sh"]) for c in range(8)], axis=0))
    qwuT = np.ascontiguousarray(np.concatenate(
        [np.asarray(res1[c]["qwuT_sh"]) for c in range(8)], axis=0))
    qwdT = np.ascontiguousarray(np.concatenate(
        [np.asarray(res1[c]["qwdT_sh"]) for c in range(8)], axis=0))
    sig = np.ascontiguousarray(np.asarray(L1_ffn_down_reorder_index).astype(np.int32))
    wot1 = np.ascontiguousarray(np.asarray(L1_attn_Wo, f32).T)

    nc = _get_nc("launch2", build_launch2, 0)
    rr = run_bass_kernel_spmd(nc, [
        {"x_in": x2[c], "qf_in": np.ascontiguousarray(np.asarray(res1[c]["q1f_out"])),
         "kf_in": np.ascontiguousarray(kv1_seq[c // 2][0:128]),
         "vf_in": np.ascontiguousarray(kv1_seq[c // 2][128:256]),
         "wot_in": wot1, "qwg_in": qwgT, "qwu_in": qwuT, "qwd_in": qwdT,
         "sig_in": sig} for c in range(8)], core_ids=list(range(8)))
    x4 = [np.asarray(r["y_out"]) for r in rr.results]
    return _uncarve(x4)
